# version 06
# baseline (speedup 1.0000x reference)
import contextlib
import numpy as np
import concourse.bass as bass
import concourse.mybir as mybir
from concourse.bass_utils import run_bass_kernel_spmd

F32 = mybir.dt.float32
BF16 = mybir.dt.bfloat16
AF = mybir.ActivationFunctionType
ALU = mybir.AluOpType

D = 2048
T = 1024
NCHUNK = 16
TH = 512
HEADS = 16
DHD = 128
KVN = 4096
SCALE = float(1.0 / np.sqrt(np.float32(DHD)))
NE, FI = 16, 768
NV, NKH, DK, DVV, CKS = 16, 8, 64, 64, 4
CN, C = 16, 64
BIGNEG = -1e30

STAGES = 4
import os
S3PH = int(os.environ.get('S3PH', '9'))
S3C = int(os.environ.get('S3C', '9'))
S3C2 = int(os.environ.get('S3C2', '9'))

SBUF_BASE = 16512
SBUF_TOP = 229344


class Em:
    ENGINES = ("pe", "act", "dve", "pool", "sp")
    _DEF = {"pe": "pe", "act": "act", "dve": "dve", "pool": "pool", "sp": "sp"}

    def __init__(self, nc, ctx):
        self.nc = nc
        self.streams = {e: [] for e in self.ENGINES}
        self.sem = {}
        for name in ("pe", "act", "dve", "pool", "sp", "dg", "ds", "da", "p0", "p1"):
            self.sem[name] = ctx.enter_context(nc.semaphore(name))
        self.cnt = {k: 0 for k in self.sem}
        self.hiwait = {e: {k: 0 for k in self.sem} for e in self.ENGINES}

    def op(self, e, fn, sem=None, inc=1):
        s = sem or self._DEF[e]
        self.cnt[s] += inc
        semh, v = self.sem[s], self.cnt[s]
        self.streams[e].append(
            lambda eng, fn=fn, semh=semh, inc=inc: fn(eng).then_inc(semh, inc))
        return (s, v)

    def dma(self, e, fn, sem):
        return self.op(e, fn, sem=sem, inc=16)

    def wait(self, e, *tokens):
        for t in tokens:
            if t is None:
                continue
            s, v = t
            if v > self.hiwait[e][s]:
                self.hiwait[e][s] = v
                semh = self.sem[s]
                self.streams[e].append(lambda eng, semh=semh, v=v: eng.wait_ge(semh, v))

    def tok(self, sem):
        return (sem, self.cnt[sem])

    def barrier(self):
        toks = [self.tok(s) for s in self.sem]
        for e in self.ENGINES:
            self.wait(e, *toks)

    def replay(self, block):
        for e, attr in (("pe", "tensor"), ("act", "scalar"), ("dve", "vector"),
                        ("pool", "gpsimd"), ("sp", "sync")):
            stream = self.streams[e]
            getattr(block, attr)(lambda eng, stream=stream: [f(eng) for f in stream])


class Alloc:

    def __init__(self, nc):
        self.nc = nc
        self.regions = []

    def at(self, name, shape, dtype, offset, tags):
        nbytes = shape[-1] * (4 if dtype == F32 else 2)
        end = offset + nbytes
        assert SBUF_BASE <= offset and end <= SBUF_TOP, f"{name}: [{offset},{end})"
        for (n2, s2, e2, t2) in self.regions:
            if offset < e2 and s2 < end and (set(tags) & set(t2)):
                raise AssertionError(
                    f"overlap {name}[{offset},{end}) vs {n2}[{s2},{e2}) tags {tags}&{t2}")
        self.regions.append((name, offset, end, tags))
        return self.nc.alloc_sbuf_tensor_at(name, list(shape), dtype, offset=offset)


TOK0 = ("pool", 0)


def build_nc():
    nc = bass.Bass("TRN2", target_bir_lowering=False)

    def din(name, shape):
        return nc.dram_tensor(name, list(shape), F32, kind="ExternalInput").ap()

    xs = din("xs", [D, T])
    wq_s = din("wq_s", [HEADS, 128, 2048])
    wo_s = din("wo_s", [4, 128, 8192])
    kT_s = din("kT_s", [HEADS, 128, KVN])
    v_s = din("v_s", [HEADS, 128, KVN])
    wqkv_s = din("wqkv_s", [16, 128, 2048])
    wz_s = din("wz_s", [8, 128, 2048])
    wb_h = din("wb_h", [128, 256])
    wa_h = din("wa_h", [128, 256])
    rt_h = din("rt_h", [128, 256])
    wout_s = din("wout_s", [16, 128, 1024])
    gu_s = din("gu_s", [NE, 128, 24576])
    dw_s = din("dw_s", [NE, 128, 12288])
    convw_h = din("convw_h", [128, 64])
    dtb_h = din("dtb_h", [16, 1])
    alog_h = din("alog_h", [16, 1])
    nw2_h = din("nw2_h", [128, 1])
    ident_h = din("ident_h", [128, 128])
    blockones2_h = din("blockones2_h", [128, 2])
    pairind_h = din("pairind_h", [2, 128])
    expind_h = din("expind_h", [16, 2048])
    hpind_h = din("hpind_h", [16, 1024])
    hpic_h = din("hpic_h", [80, 1024])
    identpair_h = din("identpair_h", [64, 128])
    maskA_h = din("maskA_h", [64, 64])
    maskP_h = din("maskP_h", [64, 64])
    oddmask_h = din("oddmask_h", [128, 1])

    out_d = nc.dram_tensor("out", [D, T], F32, kind="ExternalOutput").ap()

    I32 = mybir.dt.int32
    nonce_h = nc.dram_tensor("nonce_h", [2, 16], I32, kind="ExternalInput").ap()
    sh_q = nc.dram_tensor("sh_q", [8, 6144], F32, addr_space="Shared")
    sh_s = nc.dram_tensor("sh_s", [8, 65536], F32, addr_space="Shared")
    sh_f = nc.dram_tensor("sh_f", [8, 16], I32, addr_space="Shared")

    al = Alloc(nc)
    P = ["all"]

    o = SBUF_BASE
    x_sb = al.at("x_sb", [128, 16 * T], F32, o, P); o += 65536
    XN_OFF = o
    xn_sb = al.at("xn_sb", [128, 16 * T], BF16, o, P); o += 32768
    ident32 = al.at("ident32", [128, 128], F32, o, P); o += 512
    identb = al.at("identb", [128, 128], BF16, o, P); o += 256
    ones32 = al.at("ones32", [128, 1], F32, o, P); o += 32
    onesb = al.at("onesb", [128, 1], BF16, o, P); o += 32
    onesrow32 = al.at("onesrow32", [1, 128], F32, o, P); o += 512
    blockones2 = al.at("blockones2", [128, 2], F32, o, P); o += 32
    pairind = al.at("pairind", [2, 128], F32, o, P); o += 512
    identpair = al.at("identpair", [64, 128], F32, o, P); o += 512
    maskA = al.at("maskA", [64, 64], F32, o, P); o += 256
    maskP = al.at("maskP", [64, 64], F32, o, P); o += 256
    convw = al.at("convw", [128, 64], F32, o, P); o += 256
    wb_sb = al.at("wb_sb", [128, 256], BF16, o, P); o += 512
    wa_sb = al.at("wa_sb", [128, 256], BF16, o, P); o += 512
    rt_sb = al.at("rt_sb", [128, 256], BF16, o, P); o += 512
    dtb = al.at("dtb", [16, 1], F32, o, P); o += 32
    negA = al.at("negA", [16, 1], F32, o, P); o += 32
    nw1p = al.at("nw1p", [128, 1], F32, o, P); o += 32
    oddmask = al.at("oddmask", [128, 1], F32, o, P); o += 32
    ones16 = al.at("ones16", [16, 64], F32, o, P); o += 256
    fconst = al.at("fconst", [128, 8], F32, o, P); o += 32
    nonce_sb = al.at("nonce_sb", [2, 16], mybir.dt.int32, o, P); o += 64
    flagrd = al.at("flagrd", [1, 16], mybir.dt.int32, o, P); o += 64
    o = (o + 31) & ~31

    m_row = al.at("m_row", [1, T], F32, o, P); o += 4096
    rstd_row = al.at("rstd_row", [1, T], F32, o, P); o += 4096
    sq_sc = al.at("sq_sc", [128, T], F32, o, P); o += 4096
    rrep = al.at("rrep", [128, T], F32, o, P); o += 4096
    ARENA = (o + 31) & ~31
    expind = al.at("expind", [16, 2048], F32, ARENA, ["s2c"])
    hpind = al.at("hpind", [16, 1024], F32, ARENA + 8192, ["s3c"])

    S2 = ARENA + 12288
    gu_sb = al.at("gu_sb", [128, 24576], BF16, S2, ["s2"])
    dw_sb = al.at("dw_sb", [128, 12288], BF16, S2 + 49152, ["s2"])
    sg_sb = al.at("sg_sb", [128, 1024], BF16, S2 + 73728, ["s2"])
    crep_sb = al.at("crep_sb", [128, 1024], F32, S2 + 75776, ["s2"])
    pcl = ARENA + 8192
    pr_t = al.at("pr_t", [128, 128], F32, pcl, ["s2"]); pcl += 512
    cmb_t = al.at("cmb_t", [128, 128], F32, pcl, ["s2"]); pcl += 512
    y_t = al.at("y_t", [128, 16], F32, pcl, ["s2"]); pcl += 64
    msk_t = al.at("msk_t", [128, 16], F32, pcl, ["s2"]); pcl += 64
    mcol = al.at("mcol", [128, 1], F32, pcl, ["s2"]); pcl += 32
    thcol = al.at("thcol", [128, 1], F32, pcl, ["s2"]); pcl += 32
    scol = al.at("scol", [128, 1], F32, pcl, ["s2"]); pcl += 32
    rcol = al.at("rcol", [128, 1], F32, pcl, ["s2"]); pcl += 32
    hc_sb = al.at("hc_sb", [128, 6144], BF16, ARENA - 16384, ["s2h"])
    lg_sb = al.at("lg_sb", [16, 1024], F32, ARENA - 16384, ["s2r"])
    cfm_sb = al.at("cfm_sb", [16, 1024], F32, ARENA - 4096, ["s2h"])

    S3 = ARENA + 12288
    qcraw_sb = al.at("qcraw_sb", [128, 16384], BF16, S3, ["s3a"])
    U_all = al.at("U_all", [128, 8192], BF16, S3, ["s3b"])
    K2_all = al.at("K2_all", [128, 8192], BF16, S3 + 16384, ["s3b"])
    o_fm = al.at("o_fm", [128, 8192], BF16, S3 + 16384, ["s3e"])
    wout_b = [al.at(f"woutb{i}", [128, 1024], BF16, S3 + i * 2048, ["s3f"]) for i in range(2)]
    qc_sb = al.at("qc_sb", [128, 16384], BF16, S3 + 32768, ["s3qc"])
    wkv_b = [al.at(f"wkvb{i}", [128, 2048], BF16, S3 + 32768 + i * 4096, ["s3w"]) for i in range(2)]
    S_all = al.at("S_all", [128, 8192], BF16, S3 + 32768 + 8192, ["s3s"])
    sz_a = al.at("sz_a", [128, 7168], BF16, S3 + 65536, ["s3"])
    RO = ARENA - 16384
    sz_b = al.at("sz_b", [128, 1024], BF16, RO, ["s3r"])
    ext_sb = al.at("ext_sb", [128, 1027], BF16, RO + 2048, ["s3conv"])
    g_sb = al.at("g_sb", [16, 1024], F32, RO + 4608, ["s3g"])
    acc_sb = al.at("acc_sb", [128, 1024], F32, RO + 4608, ["s3cv"])
    beta_sb = al.at("beta_sb", [16, 1024], BF16, RO + 8704, ["s3r"])
    bnd_sb = al.at("bnd_sb", [128, 48], F32, RO + 10752, ["s3r"])
    c_all = al.at("c_all", [16, 1024], F32, RO + 10944, ["s3r"])
    cl = ARENA
    S_sb = al.at("S_sb", [128, 512], BF16, cl, ["s3S"]); cl += 1024
    bc_c = al.at("bc_c", [16, 64], F32, cl, ["s3S"]); cl += 256
    cT2 = al.at("cT2", [128, 8], F32, cl, ["s3S"]); cl += 32
    ncT2 = al.at("ncT2", [128, 8], F32, cl, ["s3S"]); cl += 32
    bT2 = al.at("bT2", [128, 8], F32, cl, ["s3S"]); cl += 32
    bbT2 = al.at("bbT2", [128, 8], F32, cl, ["s3S"]); cl += 32
    nbT2 = al.at("nbT2", [128, 8], F32, cl, ["s3S"]); cl += 32
    ecol2 = al.at("ecol2", [128, 8], F32, cl, ["s3S"]); cl += 32
    bC2a = al.at("bC2a", [128, 128], F32, cl, ["s3S"]); cl += 512
    A_blk = [al.at(f"A_blk{i}", [128, 128], BF16, cl + i * 256, ["s3S"]) for i in range(2)]; cl += 512
    Ab_blk = [al.at(f"Ab_blk{i}", [128, 128], BF16, cl + i * 256, ["s3S"]) for i in range(2)]; cl += 512
    tmp_blk = [al.at(f"tmp_blk{i}", [128, 128], BF16, cl + i * 256, ["s3S"]) for i in range(2)]; cl += 512
    e_sb2 = [al.at(f"e_sb{i}", [128, 64], BF16, cl + i * 128, ["s3S"]) for i in range(2)]; cl += 256
    el_sb2 = [al.at(f"el_sb{i}", [128, 64], BF16, cl + i * 128, ["s3S"]) for i in range(2)]; cl += 256
    qb_sb2 = [al.at(f"qb_sb{i}", [128, 64], BF16, cl + i * 128, ["s3S"]) for i in range(2)]; cl += 256
    otm2 = [al.at(f"otm{i}", [128, 64], BF16, cl + i * 128, ["s3S"]) for i in range(2)]; cl += 256
    rcA = al.at("rcA", [80, 64], F32, cl, ["s3S"]); cl += 256
    rcP = al.at("rcP", [80, 64], F32, cl, ["s3S"]); cl += 256
    assert cl <= ARENA + 5376, cl
    hpic = al.at("hpic", [80, 1024], F32, RO + 4608, ["s3hp"])
    AT_all = al.at("AT_all", [128, 8192], BF16, XN_OFF, ["s3x"])
    WT_all = al.at("WT_all", [128, 8192], BF16, XN_OFF + 16384, ["s3x"])

    a1 = ARENA
    wstr = [al.at(f"wstr{i}", [128, 2048], BF16, a1 + i * 4096, ["s1"]) for i in range(2)]
    kh_b = [al.at(f"kh{i}", [128, KVN], BF16, a1 + 8192 + i * 8192, ["s1"]) for i in range(2)]
    vh_b = [al.at(f"vh{i}", [128, KVN], BF16, a1 + 24576 + i * 8192, ["s1"]) for i in range(2)]
    qh_b = al.at("qh", [128, T], BF16, a1 + 40960, ["s1"])
    att_c = al.at("att_c", [128, 16 * T], BF16, a1 + 43008, ["s1"])
    ae_b = [al.at(f"ae{i}", [128, TH], BF16, a1 + 75776 + i * 1024, ["s1"]) for i in range(4)]
    rs_row = al.at("rs_row", [1, T], F32, a1 + 79872, ["s1"])
    wo_b = [al.at(f"wo{i}", [128, 8192], BF16, a1 + 8192 + i * 16384, ["s1wo"]) for i in range(2)]

    ctx = contextlib.ExitStack()
    PS_T = ctx.enter_context(nc.psum_tensor("PS", [128, 4096], F32))
    PS = PS_T.ap()
    PSB = PS.bitcast(BF16)

    def bank(b, rows=128, n=512, off=0):
        return PS[0:rows, b * 512 + off: b * 512 + off + n]

    em = Em(nc, ctx)

    t_x = em.dma("sp", lambda e: e.dma_start(
        out=x_sb[:, :].rearrange("p (c m) -> p c m", m=T),
        in_=xs.rearrange("(c p) m -> p c m", p=128)), "ds")
    for sb, src in [(ident32, ident_h), (blockones2, blockones2_h),
                    (pairind, pairind_h),
                    (identpair, identpair_h), (maskA, maskA_h), (maskP, maskP_h),
                    (convw, convw_h), (dtb, dtb_h), (oddmask, oddmask_h)]:
        em.dma("sp", lambda e, sb=sb, src=src: e.dma_start(out=sb[:, :], in_=src), "ds")
    for sb, src in [(identb, ident_h), (wb_sb, wb_h), (wa_sb, wa_h), (rt_sb, rt_h)]:
        em.dma("pool", lambda e, sb=sb, src=src: e.dma_start(out=sb[:, :], in_=src), "dg")
    for tgt in (ones32, onesb, onesrow32, ones16):
        em.op("pool", lambda e, tgt=tgt: e.memset(tgt[:, :], 1.0))
    for idx, val in enumerate([1e-5, 1e-6, 1.0, 0.5, BIGNEG]):
        em.op("pool", lambda e, idx=idx, val=val: e.memset(fconst[:, idx:idx + 1], val))
        nc.const_aps.register(F32, val, fconst[:, idx:idx + 1])
    em.dma("sp", lambda e: e.dma_start(out=nonce_sb[:, :], in_=nonce_h), "ds")
    t_al = em.dma("sp", lambda e: e.dma_start(out=negA[:, :], in_=alog_h), "ds")
    t_nw = em.dma("sp", lambda e: e.dma_start(out=nw1p[:, :], in_=nw2_h), "ds")
    em.wait("act", t_al, t_nw)
    t1 = em.op("act", lambda e: e.activation(negA[:, :], negA[:, :], AF.Exp))
    em.wait("dve", t1)
    em.op("dve", lambda e: e.tensor_scalar_mul(negA[:, :], negA[:, :], -1.0))
    em.op("act", lambda e: e.add(nw1p[:, :], nw1p[:, :], 1.0))
    em.barrier()

    def emit_layernorm():
        t_x_now = em.tok("dve")
        em.wait("pe", em.tok("dve"), em.tok("ds"))
        for c in range(NCHUNK):
            for half in range(2):
                t_mean = em.op("pe", lambda e, c=c, h=half: e.matmul(
                    bank(6 + h, rows=1), ones32[:, :],
                    x_sb[:, c * T + h * TH: c * T + h * TH + TH],
                    start=(c == 0), stop=(c == NCHUNK - 1)))
        em.wait("act", t_mean)
        t_m1 = em.op("act", lambda e: e.activation(m_row[0:1, 0:TH], bank(6, rows=1), AF.Copy, scale=1.0 / D))
        t_m2 = em.op("act", lambda e: e.activation(m_row[0:1, TH:T], bank(7, rows=1), AF.Copy, scale=1.0 / D))
        em.wait("pe", t_m1, t_m2)
        for half in range(2):
            t_rep = em.op("pe", lambda e, h=half: e.matmul(
                bank(4 + h), onesrow32[:, :], m_row[0:1, h * TH:h * TH + TH], start=True, stop=True))
        em.wait("dve", t_rep, em.tok("ds"))
        t_xm = []
        for c in range(NCHUNK):
            for half in range(2):
                t_xm.append(em.op("dve", lambda e, c=c, h=half: e.tensor_sub(
                    xn_sb[:, c * T + h * TH: c * T + h * TH + TH],
                    x_sb[:, c * T + h * TH: c * T + h * TH + TH], bank(4 + h))))
        t_last_xm = t_xm[-1]
        t_sq_prev = None
        for c in range(NCHUNK):
            em.wait("dve", t_xm[2 * c + 1])
            t_sq = em.op("dve", lambda e, c=c: e.tensor_mul(
                sq_sc[:, :], xn_sb[:, c * T:(c + 1) * T], xn_sb[:, c * T:(c + 1) * T]))
            em.wait("pe", t_sq)
            for half in range(2):
                t_v = em.op("pe", lambda e, c=c, h=half: e.matmul(
                    bank(6 + h, rows=1), ones32[:, :], sq_sc[:, h * TH:h * TH + TH],
                    start=(c == 0), stop=(c == NCHUNK - 1)))
            em.wait("dve", t_v)
            t_sq_prev = t_sq
        em.wait("act", t_v)
        t_r = em.op("act", lambda e: e.activation(rstd_row[0:1, 0:TH], bank(6, rows=1), AF.Ln, scale=1.0 / D, bias=1e-5))
        t_r = em.op("act", lambda e: e.activation(rstd_row[0:1, TH:T], bank(7, rows=1), AF.Ln, scale=1.0 / D, bias=1e-5))
        em.wait("act", t_r)
        t_r = em.op("act", lambda e: e.activation(rstd_row[0:1, :], rstd_row[0:1, :], AF.Exp, scale=-0.5))
        em.wait("pe", t_r, t_last_xm)
        for half in range(2):
            t_rr = em.op("pe", lambda e, h=half: e.matmul(
                bank(4 + h), onesrow32[:, :], rstd_row[0:1, h * TH:h * TH + TH], start=True, stop=True))
        em.wait("dve", t_rr)
        tfin = []
        for c in range(NCHUNK):
            for half in range(2):
                tfin.append(em.op("dve", lambda e, c=c, h=half: e.tensor_mul(
                    xn_sb[:, c * T + h * TH: c * T + h * TH + TH],
                    xn_sb[:, c * T + h * TH: c * T + h * TH + TH], bank(4 + h))))
        return tfin

    def emit_attn_a():
        last_use = {"wq": [TOK0, TOK0], "kh": [TOK0, TOK0], "vh": [TOK0, TOK0]}
        dma_toks = {}

        def dma_head(h):
            p = h % 2
            em.wait("pool", last_use["wq"][p], last_use["kh"][p], last_use["vh"][p])
            t_wq = em.dma("pool", lambda e, h=h, p=p: e.dma_start(out=wstr[p][:, :], in_=wq_s[h]), "dg")
            t_kh = em.dma("pool", lambda e, h=h, p=p: e.dma_start(out=kh_b[p][:, :], in_=kT_s[h]), "dg")
            t_vh = em.dma("pool", lambda e, h=h, p=p: e.dma_start(out=vh_b[p][:, :], in_=v_s[h]), "dg")
            dma_toks[h] = (t_wq, t_kh, t_vh)

        dma_head(0)
        dma_head(1)
        t_concat = []
        sc_consumed = [TOK0] * 6
        ae_consumed = [TOK0] * 4
        state = {"git": 0, "qh_evac": None, "norm": TOK0, "rs": TOK0}

        def emit_q(h):
            p = h % 2
            t_wq = dma_toks[h][0]
            slots = []
            for half in range(2):
                sb = state["git"] % 6
                state["git"] += 1
                slots.append(sb)
                em.wait("pe", t_wq, sc_consumed[sb])
                for c in range(NCHUNK):
                    tq = em.op("pe", lambda e, p=p, c=c, hf=half, sb=sb: e.matmul(
                        bank(sb), wstr[p][:, c * 128:(c + 1) * 128],
                        xn_sb[:, c * T + hf * TH: c * T + hf * TH + TH],
                        start=(c == 0), stop=(c == NCHUNK - 1)))
            last_use["wq"][p] = tq
            em.wait("dve", tq)
            em.op("dve", lambda e, sb=slots[0]: e.tensor_copy(qh_b[:, 0:TH], bank(sb)))
            t_ev = em.op("dve", lambda e, sb=slots[1]: e.tensor_copy(qh_b[:, TH:T], bank(sb)))
            sc_consumed[slots[0]] = t_ev
            sc_consumed[slots[1]] = t_ev
            state["qh_evac"] = t_ev

        emit_q(0)
        for h in range(HEADS):
            p = h % 2
            t_wq, t_kh, t_vh = dma_toks[h]
            t_evacQ = state["qh_evac"]
            for th in range(2):
                t_vm = None
                exp_tok = {}
                ae_of = {}
                SKEW = 2
                for it in range(32 + SKEW):
                    if it < 32:
                        kc = it
                        sb = state["git"] % 6
                        ab = state["git"] % 4
                        state["git"] += 1
                        ae_of[kc] = ab
                        em.wait("pe", t_kh, t_evacQ, sc_consumed[sb])
                        t_s = em.op("pe", lambda e, p=p, kc=kc, sb=sb, th=th: e.matmul(
                            bank(sb), kh_b[p][:, kc * 128:(kc + 1) * 128],
                            qh_b[:, th * TH:th * TH + TH], start=True, stop=True))
                        em.wait("act", t_s, ae_consumed[ab])
                        t_e = em.op("act", lambda e, sb=sb, ab=ab: e.activation(
                            ae_b[ab][:, :], bank(sb), AF.Exp, scale=SCALE))
                        sc_consumed[sb] = t_e
                        exp_tok[kc] = t_e
                    if it >= SKEW:
                        kc = it - SKEW
                        ab = ae_of[kc]
                        em.wait("pe", exp_tok[kc], t_vh)
                        if kc == 0:
                            em.wait("pe", state["norm"], state["rs"])
                        em.op("pe", lambda e, p=p, kc=kc, ab=ab: e.matmul(
                            bank(6), vh_b[p][:, kc * 128:(kc + 1) * 128],
                            ae_b[ab][:, :], start=(kc == 0), stop=(kc == 31)))
                        t_vm = em.op("pe", lambda e, kc=kc, ab=ab: e.matmul(
                            bank(7, rows=1), onesb[:, :],
                            ae_b[ab][:, :], start=(kc == 0), stop=(kc == 31)))
                        ae_consumed[ab] = t_vm
                last_use["kh"][p] = t_vm
                last_use["vh"][p] = t_vm
                if th == 1 and h + 1 < HEADS:
                    emit_q(h + 1)
                em.wait("dve", t_vm)
                t_rs = em.op("dve", lambda e: e.reciprocal(rs_row[0:1, 0:TH], bank(7, rows=1)))
                state["rs"] = t_rs
                sb = state["git"] % 6
                state["git"] += 1
                em.wait("pe", t_rs, sc_consumed[sb])
                t_rep = em.op("pe", lambda e, sb=sb: e.matmul(
                    bank(sb), onesrow32[:, :], rs_row[0:1, 0:TH], start=True, stop=True))
                em.wait("dve", t_rep, state["norm"])
                t_rr = em.op("dve", lambda e, sb=sb: e.tensor_copy(rrep[:, 0:TH], bank(sb)))
                sc_consumed[sb] = t_rr
                em.wait("dve", t_rr)
                t_norm = em.op("dve", lambda e, h=h, th=th: e.tensor_mul(
                    att_c[:, h * T + th * TH: h * T + th * TH + TH],
                    bank(6), rrep[:, 0:TH]))
                state["norm"] = t_norm
                t_concat.append(t_norm)
            if h + 2 < HEADS:
                dma_head(h + 2)

        wo_dma = {}
        wo_last = [TOK0, TOK0]

        def dma_wo(q):
            p = q % 2
            if q < 2:
                em.wait("pool", last_use["kh"][0], last_use["kh"][1],
                        last_use["vh"][0], last_use["vh"][1])
            else:
                em.wait("pool", wo_last[p])
            wo_dma[q] = em.dma("pool", lambda e, q=q, p=p: e.dma_start(
                out=wo_b[p][:, :], in_=wo_s[q]), "dg")

        dma_wo(0)
        dma_wo(1)
        em.wait("pe", *t_concat)
        t_add = []
        for q in range(4):
            p = q % 2
            em.wait("pe", wo_dma[q])
            for ob in range(4):
                oc = q * 4 + ob
                bp = (oc % 2) * 2
                if oc >= 2:
                    em.wait("pe", t_add[2 * (oc - 2) + 1])
                for half in range(2):
                    for c in range(NCHUNK):
                        t_mm = em.op("pe", lambda e, p=p, ob=ob, c=c, hf=half, bp=bp: e.matmul(
                            bank(bp + hf), wo_b[p][:, (c * 4 + ob) * 128:(c * 4 + ob) * 128 + 128],
                            att_c[:, c * T + hf * TH: c * T + hf * TH + TH],
                            start=(c == 0), stop=(c == NCHUNK - 1)))
                wo_last[p] = t_mm
                em.wait("dve", t_mm)
                for half in range(2):
                    t_ax = em.op("dve", lambda e, oc=oc, hf=half, bp=bp: e.tensor_add(
                        x_sb[:, oc * T + hf * TH: oc * T + hf * TH + TH],
                        x_sb[:, oc * T + hf * TH: oc * T + hf * TH + TH], bank(bp + hf)))
                    t_add.append(t_ax)
            if q + 2 < 4:
                dma_wo(q + 2)
        return t_add

    AX = mybir.AxisListType.X

    def emit_ffn():
        t_ei = em.dma("sp", lambda e: e.dma_start(out=expind[:, :], in_=expind_h), "ds")
        em.wait("pe", em.tok("dve"))
        for c in range(NCHUNK):
            for half in range(2):
                t_lg = em.op("pe", lambda e, c=c, h=half: e.matmul(
                    bank(6 + h, rows=16), rt_sb[:, c * 16:(c + 1) * 16],
                    xn_sb[:, c * T + h * TH: c * T + h * TH + TH],
                    start=(c == 0), stop=(c == NCHUNK - 1)))
        em.wait("dve", t_lg)
        t_lge = em.op("dve", lambda e: e.tensor_copy(lg_sb[:, :], PS[0:16, 6 * 512:6 * 512 + 1024]))
        em.wait("pe", t_lge)
        for tt in range(8):
            t_tr = em.op("pe", lambda e, tt=tt: e.transpose(
                PS[:, 0 * 512 + tt * 16: 0 * 512 + tt * 16 + 16],
                lg_sb[0:16, tt * 128:(tt + 1) * 128], ident32[0:16, 0:16]))
        em.wait("act", t_tr)
        t_pr = em.op("act", lambda e: e.activation(pr_t[:, :], bank(0, n=128), AF.Exp))
        em.wait("dve", t_pr)
        t_cmb_tok = []
        for tt in range(8):
            pr = pr_t[:, tt * 16:(tt + 1) * 16]
            cm = cmb_t[:, tt * 16:(tt + 1) * 16]
            t0 = em.op("dve", lambda e, pr=pr: e.reduce_sum(scol[:, :], pr, AX))
            em.wait("dve", t0)
            t0 = em.op("dve", lambda e: e.reciprocal(rcol[:, :], scol[:, :]))
            em.wait("dve", t0)
            t0 = em.op("dve", lambda e, pr=pr: e.tensor_scalar_mul(pr, pr, rcol[:, 0:1]))
            em.wait("dve", t0)
            t0 = em.op("dve", lambda e, pr=pr: e.tensor_copy(y_t[:, :], pr))
            for it in range(4):
                em.wait("dve", t0)
                t0 = em.op("dve", lambda e: e.reduce_max(thcol[:, :], y_t[:, :], AX))
                if it < 3:
                    em.wait("dve", t0)
                    t0 = em.op("dve", lambda e: e.tensor_single_scalar(msk_t[:, :], y_t[:, :], thcol[:, 0:1], ALU.is_ge))
                    em.wait("dve", t0)
                    t0 = em.op("dve", lambda e: e.scalar_tensor_tensor(
                        y_t[:, :], msk_t[:, :], BIGNEG, y_t[:, :], ALU.mult, ALU.add))
            em.wait("dve", t0)
            t0 = em.op("dve", lambda e, pr=pr: e.tensor_single_scalar(msk_t[:, :], pr, thcol[:, 0:1], ALU.is_ge))
            em.wait("dve", t0)
            t0 = em.op("dve", lambda e, pr=pr: e.tensor_mul(y_t[:, :], pr, msk_t[:, :]))
            em.wait("dve", t0)
            t0 = em.op("dve", lambda e: e.reduce_sum(scol[:, :], y_t[:, :], AX))
            em.wait("dve", t0)
            t0 = em.op("dve", lambda e: e.reciprocal(rcol[:, :], scol[:, :]))
            em.wait("dve", t0)
            t0 = em.op("dve", lambda e, cm=cm: e.tensor_scalar_mul(cm, y_t[:, :], rcol[:, 0:1]))
            t_cmb_tok.append(t0)
        em.wait("pe", *t_cmb_tok)
        for tt in range(8):
            t_tc = em.op("pe", lambda e, tt=tt: e.transpose(
                PS[0:16, 512 + tt * 128: 512 + tt * 128 + 128],
                cmb_t[:, tt * 16:(tt + 1) * 16], ident32[:, :]))
        em.wait("dve", t_tc)
        t_cfm = em.op("dve", lambda e: e.tensor_copy(cfm_sb[:, :], PS[0:16, 512:512 + 1024]))

        gu3 = gu_sb[:, :].rearrange("p (dc f) -> p dc f", f=1536)
        dw3 = dw_sb[:, :].rearrange("p (fc d) -> p fc d", d=2048)
        gu_last = [em.tok("pe"), em.tok("pe")]
        dw_last = [em.tok("pe"), em.tok("pe")]
        gu_dma = {}
        dw_dma = {}

        def dma_gu(e_, half):
            em.wait("pool", gu_last[half])
            gu_dma[(e_, half)] = em.dma("pool", lambda e, e_=e_, half=half: e.dma_start(
                out=gu3[:, :, 768 * half:768 * (half + 1)],
                in_=gu_s[e_].rearrange("p (dc f) -> p dc f", f=1536)[:, :, 768 * half:768 * (half + 1)]), "dg")

        def dma_dw(e_, half):
            em.wait("pool", dw_last[half])
            dw_dma[(e_, half)] = em.dma("pool", lambda e, e_=e_, half=half: e.dma_start(
                out=dw3[:, :, 1024 * half:1024 * (half + 1)],
                in_=dw_s[e_].rearrange("p (fc d) -> p fc d", d=2048)[:, :, 1024 * half:1024 * (half + 1)]), "dg")

        dma_gu(0, 0); dma_gu(0, 1); dma_dw(0, 0); dma_dw(0, 1)
        t_down_evac = [em.tok("dve")] * 16
        em.wait("pe", t_pr)
        for ex in range(NE):
            em.wait("pe", t_cfm, t_down_evac[15])
            for half in range(2):
                t_cr = em.op("pe", lambda e, ex=ex, h=half: e.matmul(
                    bank(6 + h), expind[:, ex * 128:(ex + 1) * 128],
                    cfm_sb[:, h * TH:h * TH + TH], start=True, stop=True))
            em.wait("dve", t_cr)
            em.op("dve", lambda e: e.tensor_copy(crep_sb[:, 0:TH], bank(6)))
            t_crep = em.op("dve", lambda e: e.tensor_copy(crep_sb[:, TH:T], bank(7)))
            t_silu_prev, t_h1_prev = None, None
            t_hc = []
            for fc in range(6):
                em.wait("pe", gu_dma[(ex, 0)])
                if t_silu_prev is not None:
                    em.wait("pe", t_silu_prev)
                for half in range(2):
                    for c in range(NCHUNK):
                        t_g = em.op("pe", lambda e, fc=fc, c=c, h=half: e.matmul(
                            bank(0 + h), gu_sb[:, c * 1536 + fc * 128: c * 1536 + fc * 128 + 128],
                            xn_sb[:, c * T + h * TH: c * T + h * TH + TH],
                            start=(c == 0), stop=(c == NCHUNK - 1)))
                em.wait("act", t_g)
                t_silu_prev = em.op("act", lambda e: e.activation(sg_sb[:, :], PS[:, 0:1024], AF.Silu))
                em.wait("pe", gu_dma[(ex, 1)])
                if t_h1_prev is not None:
                    em.wait("pe", t_h1_prev)
                for half in range(2):
                    for c in range(NCHUNK):
                        t_u = em.op("pe", lambda e, fc=fc, c=c, h=half: e.matmul(
                            bank(2 + h), gu_sb[:, c * 1536 + 768 + fc * 128: c * 1536 + 768 + fc * 128 + 128],
                            xn_sb[:, c * T + h * TH: c * T + h * TH + TH],
                            start=(c == 0), stop=(c == NCHUNK - 1)))
                em.wait("dve", t_u, t_silu_prev, t_crep)
                t_h1 = em.op("dve", lambda e, fc=fc: e.tensor_mul(
                    hc_sb[:, fc * 1024:(fc + 1) * 1024], sg_sb[:, :], PS[:, 2 * 512:2 * 512 + 1024]))
                em.wait("dve", t_h1)
                t_h1 = em.op("dve", lambda e, fc=fc: e.tensor_mul(
                    hc_sb[:, fc * 1024:(fc + 1) * 1024], hc_sb[:, fc * 1024:(fc + 1) * 1024], crep_sb[:, :]))
                t_h1_prev = t_h1
                t_hc.append(t_h1)
            gu_last[0] = t_g
            gu_last[1] = t_u
            if ex + 1 < NE:
                dma_gu(ex + 1, 0); dma_gu(ex + 1, 1)
            em.wait("pe", *t_hc)
            for dc in range(NCHUNK):
                bp = 4 + (dc % 2) * 2
                em.wait("pe", dw_dma[(ex, 0 if dc < 8 else 1)])
                if dc >= 2:
                    em.wait("pe", t_down_evac[dc - 2])
                else:
                    em.wait("pe", t_crep)
                for half in range(2):
                    for fc in range(6):
                        t_d = em.op("pe", lambda e, dc=dc, fc=fc, h=half, bp=bp: e.matmul(
                            bank(bp + h), dw_sb[:, fc * 2048 + dc * 128: fc * 2048 + dc * 128 + 128],
                            hc_sb[:, fc * 1024 + h * TH: fc * 1024 + h * TH + TH],
                            start=(fc == 0), stop=(fc == 5)))
                em.wait("dve", t_d)
                for half in range(2):
                    t_da = em.op("dve", lambda e, dc=dc, h=half, bp=bp: e.tensor_add(
                        x_sb[:, dc * T + h * TH: dc * T + h * TH + TH],
                        x_sb[:, dc * T + h * TH: dc * T + h * TH + TH], bank(bp + h)))
                t_down_evac[dc] = t_da
                if dc == 7:
                    dw_last[0] = t_d
                    if ex + 1 < NE:
                        dma_dw(ex + 1, 0)
            dw_last[1] = t_d
            if ex + 1 < NE:
                dma_dw(ex + 1, 1)

    PID = nc.partition_id(engines=[mybir.EngineType.Pool])
    SRC = PID - (PID % 2)

    def emit_exchange(k, sh_t, row_elems, pat, src3, dst3):
        my_off = PID * row_elems
        pr_off = SRC * row_elems
        em.dma("pool", lambda e: e.dma_start(
            out=bass.AP(sh_t, my_off, pat), in_=src3), "dg")
        em.wait("pool", em.tok("dg"))
        em.dma("pool", lambda e, k=k: e.dma_start(
            out=bass.AP(sh_f, PID * 16, [[16, 1], [1, 16]]), in_=nonce_sb[k:k + 1, :]), "dg")
        psem = em.sem["p%d" % k]

        def pollfn(g):
            with g.register(f"rw{k}") as rw, g.register(f"rf{k}") as rf, \
                 g.register(f"rx{k}") as rx, g.register(f"rc{k}") as rc:
                g.reg_load(rx, nonce_sb[k:k + 1, 0:1])
                g.reg_mov(rw, 0)
                g.reg_mov(rc, 1)
                with g.While(rc):
                    g.reg_add(rw, rw, 16)
                    g.dma_start(out=flagrd[0:1, 0:1],
                                in_=bass.AP(sh_f, SRC * 16, [[16, 1], [1, 1]])).then_inc(psem, 16)
                    g.wait_ge(psem, rw)
                    g.reg_load(rf, flagrd[0:1, 0:1])
                    g.reg_sub(rc, rf, rx)
            return g.nop()

        em.op("pool", pollfn)
        em.dma("pool", lambda e: e.dma_start(
            out=dst3, in_=bass.AP(sh_t, pr_off, pat)), "dg")

    def emit_attn_b():
        em.dma("sp", lambda e: e.dma_start(out=hpind[:, :], in_=hpind_h), "ds")
        pairind_b = al.at("pairind_b", [2, 128], BF16, ARENA + 7680, ["s3S"])
        l2r = al.at("l2r", [2, 1024], BF16, ARENA + 5376, ["s3S"])
        bT2x = al.at("bT2x", [128, 8], F32, ARENA + 7424, ["s3S"])
        em.dma("pool", lambda e: e.dma_start(out=pairind_b[:, :], in_=pairind_h), "dg")

        def sz(j):
            return sz_a[:, j * 1024:(j + 1) * 1024] if j < 7 else sz_b[:, :]

        em.wait("pe", em.tok("dve"))
        for c in range(NCHUNK):
            for half in range(2):
                t_b = em.op("pe", lambda e, c=c, h=half: e.matmul(
                    bank(6 + h, rows=16), wb_sb[:, c * 16:(c + 1) * 16],
                    xn_sb[:, c * T + h * TH: c * T + h * TH + TH],
                    start=(c == 0), stop=(c == NCHUNK - 1)))
        em.wait("act", t_b)
        t_beta = em.op("act", lambda e: e.activation(
            beta_sb[:, :], PS[0:16, 6 * 512:6 * 512 + 1024], AF.Sigmoid))
        em.wait("pe", t_beta)
        for c in range(NCHUNK):
            for half in range(2):
                t_g = em.op("pe", lambda e, c=c, h=half: e.matmul(
                    bank(6 + h, rows=16), wa_sb[:, c * 16:(c + 1) * 16],
                    xn_sb[:, c * T + h * TH: c * T + h * TH + TH],
                    start=(c == 0), stop=(c == NCHUNK - 1)))
        em.wait("dve", t_g)
        t0 = em.op("dve", lambda e: e.tensor_scalar_add(
            g_sb[:, :], PS[0:16, 6 * 512:6 * 512 + 1024], dtb[:, 0:1]))
        em.wait("act", t0)
        t1 = em.op("act", lambda e: e.activation(c_all[:, :], g_sb[:, :], AF.Abs))
        em.wait("act", t1)
        t1 = em.op("act", lambda e: e.activation(c_all[:, :], c_all[:, :], AF.Exp, scale=-1.0))
        em.wait("act", t1)
        t1 = em.op("act", lambda e: e.activation(c_all[:, :], c_all[:, :], AF.Ln, bias=1.0))
        em.wait("dve", t1)
        t0 = em.op("dve", lambda e: e.tensor_relu(g_sb[:, :], g_sb[:, :]))
        em.wait("dve", t0)
        t0 = em.op("dve", lambda e: e.tensor_add(g_sb[:, :], g_sb[:, :], c_all[:, :]))
        em.wait("dve", t0)
        t0 = em.op("dve", lambda e: e.tensor_single_scalar(g_sb[:, :], g_sb[:, :], negA[:, 0:1], ALU.mult))
        t_scan = []
        em.wait("dve", t0)
        for n in range(CN):
            t_scan.append(em.op("dve", lambda e, n=n: e.tensor_tensor_scan(
                c_all[:, n * C:(n + 1) * C], ones16[:, :], g_sb[:, n * C:(n + 1) * C],
                0.0, ALU.mult, ALU.add)))

        z_dma = {}
        z_last = [TOK0, TOK0]

        def dma_z(ocz):
            p = ocz % 2
            em.wait("pool", z_last[p])
            z_dma[ocz] = em.dma("pool", lambda e, ocz=ocz, p=p: e.dma_start(
                out=wkv_b[p][:, :], in_=wz_s[ocz]), "dg")

        dma_z(0); dma_z(1)
        t_sz_prev = None
        for ocz in range(8):
            p = ocz % 2
            em.wait("pe", z_dma[ocz])
            if t_sz_prev is not None:
                em.wait("pe", t_sz_prev)
            for half in range(2):
                for c in range(NCHUNK):
                    t_z = em.op("pe", lambda e, p=p, c=c, h=half: e.matmul(
                        bank(2 + h), wkv_b[p][:, c * 128:(c + 1) * 128],
                        xn_sb[:, c * T + h * TH: c * T + h * TH + TH],
                        start=(c == 0), stop=(c == NCHUNK - 1)))
            z_last[p] = t_z
            if ocz + 2 < 8:
                dma_z(ocz + 2)
            em.wait("act", t_z)
            t_sz_prev = em.op("act", lambda e, ocz=ocz: e.activation(
                sz(ocz), PS[:, 2 * 512:2 * 512 + 1024], AF.Silu))

        q_dma = {}

        def dma_qkv(ocq):
            p = ocq % 2
            em.wait("pool", z_last[p] if ocq < 2 else q_dma_last[p])
            q_dma[ocq] = em.dma("pool", lambda e, ocq=ocq, p=p: e.dma_start(
                out=wkv_b[p][:, :], in_=wqkv_s[ocq]), "dg")

        q_dma_last = [TOK0, TOK0]
        dma_qkv(0); dma_qkv(1)
        t_qe_prev = None
        t_bnd = []
        for ocq in range(16):
            p = ocq % 2
            em.wait("pe", q_dma[ocq])
            if t_qe_prev is not None:
                em.wait("pe", t_qe_prev)
            for half in range(2):
                for c in range(NCHUNK):
                    t_q = em.op("pe", lambda e, p=p, c=c, h=half: e.matmul(
                        bank(0 + h), wkv_b[p][:, c * 128:(c + 1) * 128],
                        xn_sb[:, c * T + h * TH: c * T + h * TH + TH],
                        start=(c == 0), stop=(c == NCHUNK - 1)))
            q_dma_last[p] = t_q
            if ocq + 2 < 16:
                dma_qkv(ocq + 2)
            em.wait("dve", t_q)
            em.op("dve", lambda e, ocq=ocq: e.tensor_copy(
                qcraw_sb[:, ocq * 1024:(ocq + 1) * 1024], PS[:, 0:1024]))
            t_qe_prev = em.op("dve", lambda e, ocq=ocq: e.tensor_copy(
                bnd_sb[:, ocq * 3:ocq * 3 + 3], PS[:, 1021:1024]))
            t_bnd.append(t_qe_prev)
        em.barrier()
        if S3PH < 2:
            return

        emit_exchange(0, sh_q, 6144, [[3, 128], [384, 16], [1, 3]],
                      bnd_sb[:, :].rearrange("p (c m) -> p c m", m=3),
                      bnd_sb[:, :].rearrange("p (c m) -> p c m", m=3))
        em.wait("dve", em.tok("dg"))
        t_bm = em.op("dve", lambda e: e.tensor_scalar_mul(bnd_sb[:, :], bnd_sb[:, :], oddmask[:, 0:1]))
        em.barrier()
        if S3PH < 3:
            return

        for oc in range(16):
            em.wait("dve", t_bm)
            t0 = em.op("dve", lambda e, oc=oc: e.tensor_copy(ext_sb[:, 0:3], bnd_sb[:, oc * 3:oc * 3 + 3]))
            em.wait("dve", t0)
            t0 = em.op("dve", lambda e, oc=oc: e.tensor_copy(
                ext_sb[:, 3:1027], qcraw_sb[:, oc * 1024:(oc + 1) * 1024]))
            em.wait("dve", t0)
            t0 = em.op("dve", lambda e, oc=oc: e.tensor_single_scalar(
                acc_sb[:, :], ext_sb[:, 0:1024], convw[:, oc * 4:oc * 4 + 1], ALU.mult))
            for k in range(1, 4):
                em.wait("dve", t0)
                t0 = em.op("dve", lambda e, oc=oc, k=k: e.scalar_tensor_tensor(
                    acc_sb[:, :], ext_sb[:, k:k + 1024], convw[:, oc * 4 + k:oc * 4 + k + 1],
                    acc_sb[:, :], ALU.mult, ALU.add))
            em.wait("act", t0)
            t_cv = em.op("act", lambda e, oc=oc: e.activation(
                qc_sb[:, oc * 1024:(oc + 1) * 1024], acc_sb[:, :], AF.Silu))
            em.wait("dve", t_cv)
        QLN = float(np.log(1.0 / 8.0))
        if not hasattr(nc, "_qln_reg"):
            nc._qln_reg = True
            nc.const_aps.register(F32, QLN, fconst[:, 5:6])
        t_qlnm = em.op("pool", lambda e: e.memset(fconst[:, 5:6], QLN))
        em.wait("act", t_qlnm)
        for kt in range(8):
            em.wait("dve", em.tok("act"))
            t0 = em.op("dve", lambda e, kt=kt: e.tensor_mul(
                acc_sb[:, :], qc_sb[:, kt * 1024:(kt + 1) * 1024], qc_sb[:, kt * 1024:(kt + 1) * 1024]))
            em.wait("pe", t0)
            for half in range(2):
                t_s = em.op("pe", lambda e, h=half: e.matmul(
                    bank(6 + h, rows=2), blockones2[:, :], acc_sb[:, h * TH:h * TH + TH],
                    start=True, stop=True))
            em.wait("act", t_s)
            t_l = em.op("act", lambda e: e.activation(
                PS[0:2, 5 * 512:5 * 512 + 1024], PS[0:2, 6 * 512:6 * 512 + 1024], AF.Ln, bias=1e-6))
            em.wait("act", t_l)
            t_l = em.op("act", lambda e, kt=kt: e.activation(
                l2r[:, :], PS[0:2, 5 * 512:5 * 512 + 1024], AF.Exp, scale=-0.5,
                bias=(QLN if kt < 4 else 0.0)))
            em.wait("pe", t_l)
            for half in range(2):
                t_rp = em.op("pe", lambda e, h=half: e.matmul(
                    bank(0 + h), pairind_b[:, :], l2r[:, h * TH:h * TH + TH], start=True, stop=True))
            em.wait("dve", t_rp)
            t_fin = em.op("dve", lambda e, kt=kt: e.tensor_mul(
                qc_sb[:, kt * 1024:(kt + 1) * 1024], qc_sb[:, kt * 1024:(kt + 1) * 1024],
                PS[:, 0:1024]))
            em.wait("pe", t_fin)
            em.wait("dve", t_fin)
        em.barrier()

        em.wait("sp", em.tok("act"), em.tok("dve"), em.tok("pe"))
        em.dma("sp", lambda e: e.dma_start(out=hpic[:, :], in_=hpic_h), "ds")
        em.wait("dve", em.tok("ds"))
        em.wait("pe", em.tok("ds"))
        em.op("dve", lambda e: e.tensor_copy(rcA[0:64, :], maskA[:, :]))
        em.op("dve", lambda e: e.tensor_copy(rcP[0:64, :], maskP[:, :]))
        for par in range(2):
            em.op("pool", lambda e, par=par: e.memset(A_blk[par][0:64, 64:128], 0.0))
            em.op("pool", lambda e, par=par: e.memset(A_blk[par][64:128, 0:64], 0.0))
            em.op("pool", lambda e, par=par: e.memset(Ab_blk[par][0:64, 64:128], 0.0))
            em.op("pool", lambda e, par=par: e.memset(Ab_blk[par][64:128, 0:64], 0.0))
        em.barrier()

        def kch(j, n):
            kt, rb = 4 + j // 2, 64 * (j % 2)
            return qc_sb[rb:rb + 64, kt * 1024 + n * C: kt * 1024 + n * C + C]

        def qch(j, n):
            kt, rb = j // 2, 64 * (j % 2)
            return qc_sb[rb:rb + 64, kt * 1024 + n * C: kt * 1024 + n * C + C]

        def vch(h, n):
            kt, rb = 8 + h // 2, 64 * (h % 2)
            return qc_sb[rb:rb + 64, kt * 1024 + n * C: kt * 1024 + n * C + C]

        def sl_all(t, n, j):
            off = (n * 8 + j) * 64
            return t[:, off:off + 64]

        if S3PH < 4:
            return
        par_last = [em.tok("dve"), em.tok("dve")]
        t_chunkpre = {}
        for n in range(CN):
            cch = c_all[:, n * C:(n + 1) * C]
            em.wait("pe", par_last[0], par_last[1])
            if n > 0:
                em.wait("pe", t_chunkpre[n - 1])
            t_ct = em.op("pe", lambda e, cch=cch: e.transpose(
                PS[0:64, 6 * 512:6 * 512 + 16], cch, ident32[0:16, 0:16]))
            t_bt = em.op("pe", lambda e, n=n: e.transpose(
                PSB[0:64, 7 * 1024:7 * 1024 + 16], beta_sb[:, n * C:(n + 1) * C], identb[0:16, 0:16]))
            for j in range(8):
                t_r63 = em.op("pe", lambda e, j=j, cch=cch: e.matmul(
                    PS[:, 7 * 512 + 16 + j: 7 * 512 + 17 + j], hpind[:, j * 128:(j + 1) * 128],
                    cch[:, 63:64], start=True, stop=True))
            em.wait("dve", t_ct, t_bt, t_r63, em.tok("act"))
            em.op("dve", lambda e, cch=cch: e.tensor_copy(rcA[64:80, :], cch))
            em.op("dve", lambda e, cch=cch: e.tensor_copy(rcP[64:80, :], cch))
            tcp = []
            for j in range(8):
                tcp.append(em.op("dve", lambda e, j=j: e.tensor_copy(
                    cT2[0:64, j:j + 1], PS[0:64, 6 * 512 + 2 * j: 6 * 512 + 2 * j + 1])))
                tcp.append(em.op("dve", lambda e, j=j: e.tensor_copy(
                    cT2[64:128, j:j + 1], PS[0:64, 6 * 512 + 2 * j + 1: 6 * 512 + 2 * j + 2])))
                tcp.append(em.op("dve", lambda e, j=j: e.tensor_copy(
                    bT2x[0:64, j:j + 1], PSB[0:64, 7 * 1024 + 2 * j: 7 * 1024 + 2 * j + 1])))
                tcp.append(em.op("dve", lambda e, j=j: e.tensor_copy(
                    bT2x[64:128, j:j + 1], PSB[0:64, 7 * 1024 + 2 * j + 1: 7 * 1024 + 2 * j + 2])))
            em.wait("dve", *tcp)
            t0 = em.op("dve", lambda e: e.tensor_scalar_mul(ncT2[:, :], cT2[:, :], -1.0))
            em.wait("act", t0)
            t_bt2 = em.op("act", lambda e: e.activation(bT2[:, :], cT2[:, :], AF.Exp))
            em.wait("dve", t_bt2)
            t0 = em.op("dve", lambda e: e.tensor_mul(bbT2[:, :], bT2x[:, :], bT2[:, :]))
            em.wait("dve", t0)
            t0 = em.op("dve", lambda e: e.tensor_scalar_mul(nbT2[:, :], bT2x[:, :], -1.0))
            em.wait("dve", t0)
            t_e2 = em.op("dve", lambda e: e.tensor_sub(
                ecol2[:, :], PS[:, 7 * 512 + 16:7 * 512 + 24], cT2[:, :]))
            em.wait("act", t_e2)
            t_bc2 = em.op("act", lambda e, n=n: e.activation(
                bC2a[:, n * 8:(n + 1) * 8], PS[:, 7 * 512 + 16:7 * 512 + 24], AF.Exp))
            t_ec2 = em.op("act", lambda e: e.activation(ecol2[:, :], ecol2[:, :], AF.Exp))
            t_chunkpre[n] = t_ec2

            for j in range(8):
                if S3C < 2:
                    break
                par = j % 2
                B0, B1, B2 = 3 * par, 3 * par + 1, 3 * par + 2
                h1, h2 = 2 * j, 2 * j + 1
                em.wait("pe", par_last[par], t_ec2)
                for rb in (0, 64):
                    em.op("pe", lambda e, j=j, n=n, rb=rb, B0=B0: e.matmul(
                        PS[rb:rb + 64, B0 * 512:B0 * 512 + 64], kch(j, n), qch(j, n),
                        start=True, stop=True))
                for rb in (0, 64):
                    em.op("pe", lambda e, j=j, n=n, rb=rb, B0=B0: e.matmul(
                        PS[rb:rb + 64, B0 * 512 + 64:B0 * 512 + 128], kch(j, n), kch(j, n),
                        start=True, stop=True))
                em.op("pe", lambda e, j=j, B1=B1: e.matmul(
                    PS[:, B1 * 512:B1 * 512 + 64], hpic[:, j * 128:(j + 1) * 128], rcA[:, :],
                    start=True, stop=True))
                em.op("pe", lambda e, j=j, B1=B1: e.matmul(
                    PS[:, B1 * 512 + 64:B1 * 512 + 128], hpic[:, j * 128:(j + 1) * 128], rcP[:, :],
                    start=True, stop=True))
                rbk = 64 * (j % 2)
                for rb in (0, 64):
                    em.op("pe", lambda e, j=j, n=n, rb=rb, B0=B0, rbk=rbk: e.transpose(
                        PSB[rb:rb + 64, B0 * 1024 + 512:B0 * 1024 + 576], kch(j, n),
                        identb[rbk:rbk + 64, rbk:rbk + 64]))
                em.op("pe", lambda e, j=j, n=n, B1=B1: e.transpose(
                    PSB[0:64, B1 * 1024 + 512:B1 * 1024 + 576], vch(2 * j, n),
                    identb[0:64, 0:64]))
                t_pe1 = em.op("pe", lambda e, j=j, n=n, B1=B1: e.transpose(
                    PSB[64:128, B1 * 1024 + 512:B1 * 1024 + 576], vch(2 * j + 1, n),
                    identb[64:128, 64:128]))
                em.wait("act", t_pe1)
                t_eT = em.op("act", lambda e, j=j, par=par, B1=B1: e.activation(
                    e_sb2[par][:, :], PS[:, B1 * 512:B1 * 512 + 64], AF.Exp,
                    bias=ncT2[:, j:j + 1]))
                t_eL = em.op("act", lambda e, j=j, par=par, B1=B1: e.activation(
                    el_sb2[par][:, :], PS[:, B1 * 512 + 64:B1 * 512 + 128], AF.Exp,
                    scale=-1.0, bias=cT2[:, j:j + 1]))
                em.wait("dve", t_eT, t_eL)
                em.op("dve", lambda e, n=n, j=j, par=par, B0=B0: e.tensor_mul(
                    sl_all(AT_all, n, j), e_sb2[par][:, :], PS[:, B0 * 512:B0 * 512 + 64]))
                for rb in (0, 64):
                    em.op("dve", lambda e, rb=rb, par=par, j=j, B0=B0: e.scalar_tensor_tensor(
                        A_blk[par][rb:rb + 64, rb:rb + 64], el_sb2[par][rb:rb + 64, :],
                        nbT2[rb:rb + 64, j:j + 1], PS[rb:rb + 64, B0 * 512 + 64:B0 * 512 + 128],
                        ALU.mult, ALU.mult))
                em.op("dve", lambda e, par=par, j=j, B1=B1: e.tensor_single_scalar(
                    tmp_blk[par][:, 0:64], PSB[:, B1 * 1024 + 512:B1 * 1024 + 576],
                    bT2x[:, j:j + 1], ALU.mult))
                em.op("dve", lambda e, par=par, j=j, B0=B0: e.tensor_single_scalar(
                    tmp_blk[par][:, 64:128], PSB[:, B0 * 1024 + 512:B0 * 1024 + 576],
                    bbT2[:, j:j + 1], ALU.mult))
                t_dv1 = em.op("dve", lambda e, n=n, j=j, par=par, B0=B0: e.tensor_single_scalar(
                    sl_all(K2_all, n, j), PSB[:, B0 * 1024 + 512:B0 * 1024 + 576],
                    ecol2[:, j:j + 1], ALU.mult))
                em.wait("pe", t_dv1)
                t_pe2 = None
                for rb in (0, 64):
                    t_pe2 = em.op("pe", lambda e, rb=rb, par=par, B2=B2: e.transpose(
                        PSB[rb:rb + 64, B2 * 1024 + 768:B2 * 1024 + 832],
                        A_blk[par][rb:rb + 64, rb:rb + 64], identb[rb:rb + 64, rb:rb + 64]))
                em.wait("dve", t_pe2)
                t_dv2 = None
                for rb in (0, 64):
                    t_dv2 = em.op("dve", lambda e, rb=rb, par=par, B2=B2: e.tensor_copy(
                        Ab_blk[par][rb:rb + 64, rb:rb + 64],
                        PSB[rb:rb + 64, B2 * 1024 + 768:B2 * 1024 + 832]))
                t_dve_level = t_dv2
                for lvl in range(6):
                    em.wait("pe", t_dve_level)
                    t_mm_t = em.op("pe", lambda e, par=par, B2=B2: e.matmul(
                        PS[:, B2 * 512:B2 * 512 + 128], Ab_blk[par][:, :], tmp_blk[par][:, :],
                        start=True, stop=True))
                    if lvl < 5:
                        em.op("pe", lambda e, par=par, B2=B2: e.matmul(
                            PS[:, B2 * 512 + 128:B2 * 512 + 256], Ab_blk[par][:, :],
                            A_blk[par][:, :], start=True, stop=True))
                        t_mm_b = em.op("pe", lambda e, par=par, B2=B2: e.matmul(
                            PS[:, B2 * 512 + 256:B2 * 512 + 384], A_blk[par][:, :],
                            Ab_blk[par][:, :], start=True, stop=True))
                    em.wait("dve", t_mm_t if lvl == 5 else t_mm_b)
                    t_tmp = em.op("dve", lambda e, par=par, B2=B2: e.tensor_add(
                        tmp_blk[par][:, :], tmp_blk[par][:, :], PS[:, B2 * 512:B2 * 512 + 128]))
                    if lvl < 5:
                        em.op("dve", lambda e, par=par, B2=B2: e.tensor_copy(
                            A_blk[par][:, :], PS[:, B2 * 512 + 128:B2 * 512 + 256]))
                        t_dve_level = em.op("dve", lambda e, par=par, B2=B2: e.tensor_copy(
                            Ab_blk[par][:, :], PS[:, B2 * 512 + 256:B2 * 512 + 384]))
                    else:
                        t_dve_level = t_tmp
                t_u = em.op("dve", lambda e, n=n, j=j, par=par: e.tensor_copy(
                    sl_all(U_all, n, j), tmp_blk[par][:, 0:64]))
                em.wait("pe", t_dve_level)
                t_pe3 = None
                for rb in (0, 64):
                    t_pe3 = em.op("pe", lambda e, rb=rb, par=par, B2=B2: e.transpose(
                        PSB[rb:rb + 64, B2 * 1024 + 768:B2 * 1024 + 832],
                        tmp_blk[par][rb:rb + 64, 64:128], identb[rb:rb + 64, rb:rb + 64]))
                em.wait("dve", t_pe3)
                t_w = None
                for rb in (0, 64):
                    t_w = em.op("dve", lambda e, rb=rb, n=n, j=j, par=par, B2=B2: e.tensor_copy(
                        sl_all(WT_all, n, j)[rb:rb + 64, :],
                        PSB[rb:rb + 64, B2 * 1024 + 768:B2 * 1024 + 832]))
                par_last[par] = t_w
        em.barrier()

        def emit_chain(save):
            bank_last = [em.tok("dve")] * 4
            for n in range(CN):
                for j in range(8):
                    pb = (j % 2) * 2
                    em.wait("pe", bank_last[pb], bank_last[pb + 1])
                    for rb in (0, 64):
                        t_p = em.op("pe", lambda e, n=n, j=j, rb=rb, pb=pb: e.matmul(
                            PS[rb:rb + 64, pb * 512:pb * 512 + 64],
                            sl_all(WT_all, n, j)[rb:rb + 64, :],
                            S_sb[rb:rb + 64, j * 64:(j + 1) * 64], start=True, stop=True))
                    em.wait("dve", t_p)
                    if save:
                        em.op("dve", lambda e, n=n, j=j: e.tensor_copy(
                            sl_all(S_all, n, j), S_sb[:, j * 64:(j + 1) * 64]))
                        t_pp = em.op("dve", lambda e, n=n, j=j, pb=pb: e.tensor_sub(
                            sl_all(U_all, n, j), sl_all(U_all, n, j),
                            PS[:, pb * 512:pb * 512 + 64]))
                        P_ap = sl_all(U_all, n, j)
                    else:
                        t_pp = em.op("dve", lambda e, n=n, j=j, pb=pb: e.tensor_sub(
                            tmp_blk[0][:, 0:64], sl_all(U_all, n, j),
                            PS[:, pb * 512:pb * 512 + 64]))
                        P_ap = tmp_blk[0][:, 0:64]
                    em.wait("pe", t_pp)
                    for rb in (0, 64):
                        t_su = em.op("pe", lambda e, n=n, j=j, rb=rb, pb=pb, P_ap=P_ap: e.matmul(
                            PS[rb:rb + 64, (pb + 1) * 512:(pb + 1) * 512 + 64],
                            sl_all(K2_all, n, j)[rb:rb + 64, :], P_ap[rb:rb + 64, :],
                            start=True, stop=True))
                    em.wait("dve", t_su)
                    t_s = em.op("dve", lambda e, n=n, j=j, pb=pb: e.scalar_tensor_tensor(
                        S_sb[:, j * 64:(j + 1) * 64], S_sb[:, j * 64:(j + 1) * 64],
                        bC2a[:, n * 8 + j:n * 8 + j + 1],
                        PS[:, (pb + 1) * 512:(pb + 1) * 512 + 64], ALU.mult, ALU.add))
                    bank_last[pb] = t_pp
                    bank_last[pb + 1] = t_s

        if S3PH < 5:
            return
        t0 = em.op("pool", lambda e: e.memset(S_sb[:, :], 0.0))
        em.wait("dve", t0)
        em.wait("pe", t0)
        emit_chain(save=False)
        em.barrier()
        emit_exchange(1, sh_s, 65536, [[64, 128], [8192, 8], [1, 64]],
                      S_sb[:, :].rearrange("p (j m) -> p j m", m=64),
                      S_sb[:, :].rearrange("p (j m) -> p j m", m=64))
        em.wait("dve", em.tok("dg"))
        em.op("dve", lambda e: e.tensor_scalar_mul(S_sb[:, :], S_sb[:, :], oddmask[:, 0:1]))
        em.barrier()
        emit_chain(save=True)
        em.barrier()

        if S3PH < 6:
            return
        par_last = [em.tok("dve"), em.tok("dve")]
        t_bcc_prev = None
        for n in range(CN):
            cch = c_all[:, n * C:(n + 1) * C]
            em.wait("act", em.tok("pe"))
            t_bcc = em.op("act", lambda e, cch=cch: e.activation(bc_c[:, :], cch, AF.Exp))
            for j in range(8):
                par = j % 2
                h1, h2 = 2 * j, 2 * j + 1
                em.wait("pe", t_bcc, par_last[par])
                t_rb = em.op("pe", lambda e, j=j, par=par: e.matmul(
                    PS[:, 6 * 512 + par * 64:6 * 512 + par * 64 + 64],
                    hpind[:, j * 128:(j + 1) * 128], bc_c[:, :], start=True, stop=True))
                em.wait("dve", t_rb)
                for rb, hh in ((0, h1), (64, h2)):
                    em.op("dve", lambda e, rb=rb, n=n, j=j, par=par: e.tensor_mul(
                        qb_sb2[par][rb:rb + 64, :], qch(j, n),
                        PS[rb:rb + 64, 6 * 512 + par * 64:6 * 512 + par * 64 + 64]))
                t_qb = em.tok("dve")
                em.wait("pe", t_qb)
                for rb in (0, 64):
                    em.op("pe", lambda e, rb=rb, n=n, j=j, par=par: e.matmul(
                        PS[rb:rb + 64, par * 512:par * 512 + 64], qb_sb2[par][rb:rb + 64, :],
                        sl_all(S_all, n, j)[rb:rb + 64, :], start=True, stop=False))
                    em.op("pe", lambda e, rb=rb, n=n, j=j, par=par: e.matmul(
                        PS[rb:rb + 64, par * 512:par * 512 + 64],
                        sl_all(AT_all, n, j)[rb:rb + 64, :],
                        sl_all(U_all, n, j)[rb:rb + 64, :], start=False, stop=True))
                t_o = em.tok("pe")
                em.wait("dve", t_o)
                t_ev = em.op("dve", lambda e, par=par: e.tensor_copy(
                    otm2[par][:, :], PS[:, par * 512:par * 512 + 64]))
                em.wait("pe", t_ev)
                for rb in (0, 64):
                    em.op("pe", lambda e, rb=rb, par=par: e.transpose(
                        PSB[rb:rb + 64, (2 + par) * 1024:(2 + par) * 1024 + 64],
                        otm2[par][rb:rb + 64, :], identb[rb:rb + 64, rb:rb + 64]))
                t_tr = em.tok("pe")
                em.wait("dve", t_tr)
                for rb in (0, 64):
                    t_of = em.op("dve", lambda e, rb=rb, n=n, j=j, par=par: e.tensor_copy(
                        o_fm[rb:rb + 64, j * 1024 + n * C: j * 1024 + n * C + C],
                        PSB[rb:rb + 64, (2 + par) * 1024:(2 + par) * 1024 + 64]))
                par_last[par] = t_of
        em.barrier()

        if S3PH < 7:
            return
        for j in range(8):
            em.wait("dve", em.tok("pe"))
            t0 = em.op("dve", lambda e, j=j: e.tensor_mul(
                acc_sb[:, :], o_fm[:, j * 1024:(j + 1) * 1024], o_fm[:, j * 1024:(j + 1) * 1024]))
            em.wait("pe", t0)
            for half in range(2):
                t_s = em.op("pe", lambda e, h=half: e.matmul(
                    bank(6 + h, rows=2), blockones2[:, :], acc_sb[:, h * TH:h * TH + TH],
                    start=True, stop=True))
            em.wait("act", t_s)
            t_l = em.op("act", lambda e: e.activation(
                PS[0:2, 5 * 512:5 * 512 + 1024], PS[0:2, 6 * 512:6 * 512 + 1024], AF.Ln,
                scale=1.0 / 64.0, bias=1e-6))
            em.wait("act", t_l)
            t_l = em.op("act", lambda e: e.activation(
                l2r[:, :], PS[0:2, 5 * 512:5 * 512 + 1024], AF.Exp, scale=-0.5))
            em.wait("pe", t_l)
            for half in range(2):
                t_rp = em.op("pe", lambda e, h=half: e.matmul(
                    bank(0 + h), pairind_b[:, :], l2r[:, h * TH:h * TH + TH], start=True, stop=True))
            em.wait("dve", t_rp)
            t0 = em.op("dve", lambda e, j=j: e.tensor_mul(
                o_fm[:, j * 1024:(j + 1) * 1024], o_fm[:, j * 1024:(j + 1) * 1024], PS[:, 0:1024]))
            em.wait("dve", t0)
            t0 = em.op("dve", lambda e, j=j: e.tensor_single_scalar(
                o_fm[:, j * 1024:(j + 1) * 1024], o_fm[:, j * 1024:(j + 1) * 1024],
                nw1p[:, 0:1], ALU.mult))
            em.wait("dve", t0)
            t0 = em.op("dve", lambda e, j=j: e.tensor_mul(
                o_fm[:, j * 1024:(j + 1) * 1024], o_fm[:, j * 1024:(j + 1) * 1024], sz(j)))
            em.wait("pe", t0)
        wo_dma2 = {}
        wo_last2 = [em.tok("pe"), em.tok("pe")]

        def dma_wout(ocw):
            p = ocw % 2
            em.wait("pool", wo_last2[p])
            wo_dma2[ocw] = em.dma("pool", lambda e, ocw=ocw, p=p: e.dma_start(
                out=wout_b[p][:, :], in_=wout_s[ocw]), "dg")

        dma_wout(0); dma_wout(1)
        t_addf = {}
        for ocw in range(16):
            p = ocw % 2
            bp = (ocw % 2) * 2 + 2
            em.wait("pe", wo_dma2[ocw])
            if ocw >= 2:
                em.wait("pe", t_addf[ocw - 2])
            for half in range(2):
                for vc in range(8):
                    t_mm = em.op("pe", lambda e, p=p, vc=vc, hf=half, bp=bp: e.matmul(
                        bank(bp + hf), wout_b[p][:, vc * 128:(vc + 1) * 128],
                        o_fm[:, vc * 1024 + hf * TH: vc * 1024 + hf * TH + TH],
                        start=(vc == 0), stop=(vc == 7)))
            wo_last2[p] = t_mm
            if ocw + 2 < 16:
                dma_wout(ocw + 2)
            em.wait("dve", t_mm)
            for half in range(2):
                t_ax = em.op("dve", lambda e, ocw=ocw, hf=half, bp=bp: e.tensor_add(
                    x_sb[:, ocw * T + hf * TH: ocw * T + hf * TH + TH],
                    x_sb[:, ocw * T + hf * TH: ocw * T + hf * TH + TH], bank(bp + hf)))
            t_addf[ocw] = t_ax

    emit_layernorm()
    em.barrier()
    emit_attn_a()
    em.barrier()
    if STAGES >= 2:
        emit_layernorm()
        em.barrier()
        emit_ffn()
        em.barrier()
    if STAGES >= 3:
        emit_layernorm()
        em.barrier()
        emit_attn_b()
        em.barrier()
    if STAGES >= 4:
        emit_layernorm()
        em.barrier()
        emit_ffn()
        em.barrier()

    em.wait("sp", em.tok("dve"))
    t_out = em.dma("sp", lambda e: e.dma_start(
        out=out_d.rearrange("(c p) m -> p c m", p=128),
        in_=x_sb[:, :].rearrange("p (c m) -> p c m", m=T)), "ds")
    em.wait("sp", t_out)

    with nc.Block() as block:
        em.replay(block)
    ctx.close()
    return nc


def _prep_shared(i):
    f32 = lambda a: np.ascontiguousarray(np.asarray(a), dtype=np.float32)
    Wq = np.asarray(i["L0_attn_a_Wq"])
    wq_s = f32(Wq.reshape(HEADS, 128, 16, 128).transpose(0, 3, 2, 1).reshape(HEADS, 128, 2048))
    Wo = np.asarray(i["L0_attn_a_Wo"])
    wo_s = f32(Wo.reshape(4, 4, 128, 16, 128).transpose(0, 4, 3, 1, 2).reshape(4, 128, 8192))
    kv_k = np.asarray(i["L0_attn_a_kv_k"]).reshape(KVN, D)
    kT_s = f32(kv_k.T.reshape(HEADS, 128, KVN))
    kv_v = np.asarray(i["L0_attn_a_kv_v"]).reshape(KVN, D)
    v_s = f32(kv_v.reshape(32, 128, 16, 128).transpose(2, 1, 0, 3).reshape(HEADS, 128, KVN))
    Wqkv = np.asarray(i["L0_attn_b_W_qkv"])
    wqkv_s = f32(Wqkv.reshape(16, 128, 16, 128).transpose(0, 3, 2, 1).reshape(16, 128, 2048))
    Wz = np.asarray(i["L0_attn_b_W_z"])
    wz_s = f32(Wz.reshape(8, 128, 16, 128).transpose(0, 3, 2, 1).reshape(8, 128, 2048))
    wb_hh = f32(np.asarray(i["L0_attn_b_W_b"]).reshape(16, 16, 128).transpose(2, 1, 0).reshape(128, 256))
    wa_hh = f32(np.asarray(i["L0_attn_b_W_a"]).reshape(16, 16, 128).transpose(2, 1, 0).reshape(128, 256))
    rt_hh = f32(np.asarray(i["L0_ffn_router_weight"]).reshape(16, 16, 128).transpose(2, 1, 0).reshape(128, 256))
    Wout = np.asarray(i["L0_attn_b_W_out"])
    wout_s = f32(Wout.reshape(16, 128, 8, 128).transpose(0, 3, 2, 1).reshape(16, 128, 1024))
    gup = np.asarray(i["L0_ffn_gate_up_proj"])
    gu_s = f32(gup.reshape(NE, 16, 128, 1536).transpose(0, 2, 1, 3).reshape(NE, 128, 24576))
    dwn = np.asarray(i["L0_ffn_down_proj"])
    dw_s = f32(dwn.reshape(NE, 6, 128, 2048).transpose(0, 2, 1, 3).reshape(NE, 128, 12288))
    convw_hh = f32(np.asarray(i["L0_attn_b_conv_w"]).reshape(16, 128, 4).transpose(1, 0, 2).reshape(128, 64))
    dtb_hh = f32(np.asarray(i["L0_attn_b_dt_bias"]).reshape(16, 1))
    alog_hh = f32(np.asarray(i["L0_attn_b_A_log"]).reshape(16, 1))
    nw2_hh = f32(np.tile(np.asarray(i["L0_attn_b_norm_weight"]).reshape(64), 2).reshape(128, 1))
    ident = np.eye(128, dtype=np.float32)
    blockones2 = np.zeros((128, 2), np.float32)
    blockones2[:64, 0] = 1.0
    blockones2[64:, 1] = 1.0
    pairind = np.zeros((2, 128), np.float32)
    pairind[0, :64] = 1.0
    pairind[1, 64:] = 1.0
    expind = np.zeros((16, 2048), np.float32)
    for e in range(16):
        expind[e, e * 128:(e + 1) * 128] = 1.0
    hpind = np.zeros((16, 1024), np.float32)
    for j in range(8):
        hpind[2 * j, j * 128: j * 128 + 64] = 1.0
        hpind[2 * j + 1, j * 128 + 64: (j + 1) * 128] = 1.0
    hpic = np.zeros((80, 1024), np.float32)
    for j in range(8):
        for k in range(64):
            hpic[k, j * 128 + k] = 1.0
            hpic[k, j * 128 + 64 + k] = 1.0
        hpic[64 + 2 * j, j * 128: j * 128 + 64] = 1.0
        hpic[64 + 2 * j + 1, j * 128 + 64: (j + 1) * 128] = 1.0
    identpair = np.zeros((64, 128), np.float32)
    for k in range(64):
        identpair[k, k] = 1.0
        identpair[k, 64 + k] = 1.0
    ii, jj = np.meshgrid(np.arange(64), np.arange(64), indexing="ij")
    maskA_ = np.where(ii <= jj, 0.0, BIGNEG).astype(np.float32)
    maskP_ = np.where(jj >= ii, -BIGNEG, 0.0).astype(np.float32)
    return dict(wq_s=wq_s, wo_s=wo_s, kT_s=kT_s, v_s=v_s, wqkv_s=wqkv_s, wz_s=wz_s,
                wb_h=wb_hh, wa_h=wa_hh, rt_h=rt_hh, wout_s=wout_s, gu_s=gu_s, dw_s=dw_s,
                convw_h=convw_hh, dtb_h=dtb_hh, alog_h=alog_hh, nw2_h=nw2_hh,
                ident_h=ident, blockones2_h=blockones2, pairind_h=pairind,
                expind_h=expind, hpind_h=hpind, hpic_h=hpic, identpair_h=identpair,
                maskA_h=maskA_, maskP_h=maskP_)


def kernel(**inputs):
    shared = _prep_shared(inputs)
    x = np.asarray(inputs["L0_attn_a_query"], dtype=np.float32)
    base = int(np.random.randint(1, 2 ** 30))
    nonce = np.empty((2, 16), np.int32)
    nonce[0, :] = base + 1
    nonce[1, :] = base + 2
    nc = build_nc()
    in_maps = []
    for c in range(8):
        b, hh = c // 2, c % 2
        m = dict(shared)
        m["xs"] = np.ascontiguousarray(x[b, hh * T:(hh + 1) * T, :].T)
        m["oddmask_h"] = np.full((128, 1), 1.0 if hh == 1 else 0.0, np.float32)
        m["nonce_h"] = nonce
        in_maps.append(m)
    res = run_bass_kernel_spmd(nc, in_maps, core_ids=list(range(8)))
    out = np.empty((4, 2048, 2048), np.float32)
    for c in range(8):
        b, hh = c // 2, c % 2
        out[b, hh * T:(hh + 1) * T, :] = res.results[c]["out"].T
    return out


if __name__ == "__main__":
    import reference as R
    ins = R.setup_inputs()
    o = kernel(**{k: np.asarray(v) for k, v in ins.items()})
    print("ran", o.shape)


# version 10 (final)
# speedup vs baseline: 1.0769x; 1.0769x over previous
import contextlib
import numpy as np
import concourse.bass as bass
import concourse.mybir as mybir
from concourse.bass_utils import run_bass_kernel_spmd

F32 = mybir.dt.float32
BF16 = mybir.dt.bfloat16
AF = mybir.ActivationFunctionType
ALU = mybir.AluOpType

D = 2048
T = 1024
NCHUNK = 16
TH = 512
HEADS = 16
DHD = 128
KVN = 4096
SCALE = float(1.0 / np.sqrt(np.float32(DHD)))
NE, FI = 16, 768
NV, NKH, DK, DVV, CKS = 16, 8, 64, 64, 4
CN, C = 16, 64
BIGNEG = -1e30

STAGES = 4
import os
S3PH = int(os.environ.get('S3PH', '9'))
S3C = int(os.environ.get('S3C', '9'))
S3C2 = int(os.environ.get('S3C2', '9'))

SBUF_BASE = 16512
SBUF_TOP = 229344


class Em:
    ENGINES = ("pe", "act", "dve", "pool", "sp")
    _DEF = {"pe": "pe", "act": "act", "dve": "dve", "pool": "pool", "sp": "sp"}

    def __init__(self, nc, ctx):
        self.nc = nc
        self.streams = {e: [] for e in self.ENGINES}
        self.sem = {}
        for name in ("pe", "act", "dve", "pool", "sp", "dg", "ds", "da", "p0", "p1"):
            self.sem[name] = ctx.enter_context(nc.semaphore(name))
        self.cnt = {k: 0 for k in self.sem}
        self.hiwait = {e: {k: 0 for k in self.sem} for e in self.ENGINES}

    def op(self, e, fn, sem=None, inc=1):
        s = sem or self._DEF[e]
        self.cnt[s] += inc
        semh, v = self.sem[s], self.cnt[s]
        self.streams[e].append(
            lambda eng, fn=fn, semh=semh, inc=inc: fn(eng).then_inc(semh, inc))
        return (s, v)

    def dma(self, e, fn, sem):
        return self.op(e, fn, sem=sem, inc=16)

    def wait(self, e, *tokens):
        for t in tokens:
            if t is None:
                continue
            s, v = t
            if v > self.hiwait[e][s]:
                self.hiwait[e][s] = v
                semh = self.sem[s]
                self.streams[e].append(lambda eng, semh=semh, v=v: eng.wait_ge(semh, v))

    def tok(self, sem):
        return (sem, self.cnt[sem])

    def barrier(self):
        toks = [self.tok(s) for s in self.sem]
        for e in self.ENGINES:
            self.wait(e, *toks)

    def replay(self, block):
        for e, attr in (("pe", "tensor"), ("act", "scalar"), ("dve", "vector"),
                        ("pool", "gpsimd"), ("sp", "sync")):
            stream = self.streams[e]
            getattr(block, attr)(lambda eng, stream=stream: [f(eng) for f in stream])


class Alloc:

    def __init__(self, nc):
        self.nc = nc
        self.regions = []

    def at(self, name, shape, dtype, offset, tags):
        nbytes = shape[-1] * (4 if dtype == F32 else 2)
        end = offset + nbytes
        assert SBUF_BASE <= offset and end <= SBUF_TOP, f"{name}: [{offset},{end})"
        for (n2, s2, e2, t2) in self.regions:
            if offset < e2 and s2 < end and (set(tags) & set(t2)):
                raise AssertionError(
                    f"overlap {name}[{offset},{end}) vs {n2}[{s2},{e2}) tags {tags}&{t2}")
        self.regions.append((name, offset, end, tags))
        return self.nc.alloc_sbuf_tensor_at(name, list(shape), dtype, offset=offset)


TOK0 = ("pool", 0)


def build_nc():
    nc = bass.Bass("TRN2", target_bir_lowering=False)

    def din(name, shape):
        return nc.dram_tensor(name, list(shape), F32, kind="ExternalInput").ap()

    xs = din("xs", [D, T])
    wq_s = din("wq_s", [HEADS, 128, 2048])
    wo_s = din("wo_s", [4, 128, 8192])
    kT_s = din("kT_s", [HEADS, 128, KVN])
    v_s = din("v_s", [HEADS, 128, KVN])
    wqkv_s = din("wqkv_s", [16, 128, 2048])
    wz_s = din("wz_s", [8, 128, 2048])
    wb_h = din("wb_h", [128, 256])
    wa_h = din("wa_h", [128, 256])
    rt_h = din("rt_h", [128, 256])
    wout_s = din("wout_s", [16, 128, 1024])
    gu_s = din("gu_s", [NE, 128, 24576])
    dw_s = din("dw_s", [NE, 128, 12288])
    convw_h = din("convw_h", [128, 64])
    dtb_h = din("dtb_h", [16, 1])
    alog_h = din("alog_h", [16, 1])
    nw2_h = din("nw2_h", [128, 1])
    ident_h = din("ident_h", [128, 128])
    blockones2_h = din("blockones2_h", [128, 2])
    pairind_h = din("pairind_h", [2, 128])
    expind_h = din("expind_h", [16, 2048])
    hpind_h = din("hpind_h", [16, 1024])
    hpic_h = din("hpic_h", [80, 1024])
    identpair_h = din("identpair_h", [64, 128])
    maskA_h = din("maskA_h", [64, 64])
    maskP_h = din("maskP_h", [64, 64])
    oddmask_h = din("oddmask_h", [128, 1])

    out_d = nc.dram_tensor("out", [D, T], F32, kind="ExternalOutput").ap()
    dbg_d = nc.dram_tensor("dbg", [6, 128, 8192], F32, kind="ExternalOutput").ap()

    I32 = mybir.dt.int32
    nonce_h = nc.dram_tensor("nonce_h", [2, 16], I32, kind="ExternalInput").ap()
    sh_q = nc.dram_tensor("sh_q", [8, 6144], F32, addr_space="Shared")
    sh_s = nc.dram_tensor("sh_s", [8, 65536], F32, addr_space="Shared")
    sh_f = nc.dram_tensor("sh_f", [8, 16], I32, addr_space="Shared")

    al = Alloc(nc)
    P = ["all"]

    o = SBUF_BASE
    x_sb = al.at("x_sb", [128, 16 * T], F32, o, P); o += 65536
    XN_OFF = o
    xn_sb = al.at("xn_sb", [128, 16 * T], BF16, o, P); o += 32768
    ident32 = al.at("ident32", [128, 128], F32, o, P); o += 512
    identb = al.at("identb", [128, 128], BF16, o, P); o += 256
    ones32 = al.at("ones32", [128, 1], F32, o, P); o += 32
    onesb = al.at("onesb", [128, 1], BF16, o, P); o += 32
    onesrow32 = al.at("onesrow32", [1, 128], F32, o, P); o += 512
    blockones2 = al.at("blockones2", [128, 2], F32, o, P); o += 32
    pairind = al.at("pairind", [2, 128], F32, o, P); o += 512
    identpair = al.at("identpair", [64, 128], F32, o, P); o += 512
    maskA = al.at("maskA", [64, 64], F32, o, P); o += 256
    maskP = al.at("maskP", [64, 64], F32, o, P); o += 256
    convw = al.at("convw", [128, 64], F32, o, P); o += 256
    wb_sb = al.at("wb_sb", [128, 256], BF16, o, P); o += 512
    wa_sb = al.at("wa_sb", [128, 256], BF16, o, P); o += 512
    rt_sb = al.at("rt_sb", [128, 256], BF16, o, P); o += 512
    dtb = al.at("dtb", [16, 1], F32, o, P); o += 32
    negA = al.at("negA", [16, 1], F32, o, P); o += 32
    nw1p = al.at("nw1p", [128, 1], F32, o, P); o += 32
    oddmask = al.at("oddmask", [128, 1], F32, o, P); o += 32
    ones16 = al.at("ones16", [16, 64], F32, o, P); o += 256
    fconst = al.at("fconst", [128, 8], F32, o, P); o += 32
    nonce_sb = al.at("nonce_sb", [2, 16], mybir.dt.int32, o, P); o += 64
    flagrd = al.at("flagrd", [1, 16], mybir.dt.int32, o, P); o += 64
    o = (o + 31) & ~31

    m_row = al.at("m_row", [1, T], F32, o, P); o += 4096
    rstd_row = al.at("rstd_row", [1, T], F32, o, P); o += 4096
    sq_sc = al.at("sq_sc", [128, T], F32, o, P); o += 4096
    rrep = al.at("rrep", [128, T], F32, o, P); o += 4096
    ARENA = (o + 31) & ~31
    RREP_OFF = o - 4096
    expind = al.at("expind", [16, 2048], F32, ARENA, ["s2c"])
    hpind = al.at("hpind", [16, 1024], F32, ARENA + 8192, ["s3c"])

    S2 = ARENA + 12288
    gu_sb = al.at("gu_sb", [128, 24576], BF16, S2, ["s2"])
    dw_sb = al.at("dw_sb", [128, 12288], BF16, S2 + 49152, ["s2"])
    sg_sb = al.at("sg_sb", [128, 1024], BF16, S2 + 73728, ["s2"])
    crep_sb = al.at("crep_sb", [128, 1024], F32, S2 + 75776, ["s2"])
    pcl = ARENA + 8192
    pr_t = al.at("pr_t", [128, 128], F32, pcl, ["s2"]); pcl += 512
    cmb_t = al.at("cmb_t", [128, 128], F32, pcl, ["s2"]); pcl += 512
    y_t = al.at("y_t", [128, 16], F32, pcl, ["s2"]); pcl += 64
    msk_t = al.at("msk_t", [128, 16], F32, pcl, ["s2"]); pcl += 64
    mcol = al.at("mcol", [128, 1], F32, pcl, ["s2"]); pcl += 32
    thcol = al.at("thcol", [128, 1], F32, pcl, ["s2"]); pcl += 32
    scol = al.at("scol", [128, 1], F32, pcl, ["s2"]); pcl += 32
    rcol = al.at("rcol", [128, 1], F32, pcl, ["s2"]); pcl += 32
    hc_sb = al.at("hc_sb", [128, 6144], BF16, ARENA - 16384, ["s2h"])
    lg_sb = al.at("lg_sb", [16, 1024], F32, ARENA - 16384, ["s2r"])
    cfm_sb = al.at("cfm_sb", [16, 1024], F32, ARENA - 4096, ["s2h"])

    S3 = ARENA + 12288
    qcraw_sb = al.at("qcraw_sb", [128, 16384], BF16, S3, ["s3a"])
    U_all = al.at("U_all", [128, 8192], BF16, S3, ["s3b"])
    K2_all = al.at("K2_all", [128, 8192], BF16, S3 + 16384, ["s3b"])
    o_fm = al.at("o_fm", [128, 8192], BF16, S3 + 16384, ["s3e"])
    wout_b = [al.at(f"woutb{i}", [128, 1024], BF16, S3 + i * 2048, ["s3f"]) for i in range(2)]
    qc_sb = al.at("qc_sb", [128, 16384], BF16, S3 + 32768, ["s3qc"])
    wkv_b = [al.at(f"wkvb{i}", [128, 2048], BF16, S3 + 32768 + i * 4096, ["s3w"]) for i in range(2)]
    S_all = al.at("S_all", [128, 8192], BF16, S3 + 32768 + 8192, ["s3s"])
    sz_a = al.at("sz_a", [128, 7168], BF16, S3 + 65536, ["s3"])
    RO = ARENA - 16384
    sz_b = al.at("sz_b", [128, 1024], BF16, RO, ["s3r"])
    ext_sb = al.at("ext_sb", [128, 1027], BF16, RO + 2048, ["s3conv"])
    g_sb = al.at("g_sb", [16, 1024], F32, RO + 4608, ["s3g"])
    acc_sb = al.at("acc_sb", [128, 1024], F32, RO + 4608, ["s3cv"])
    beta_sb = al.at("beta_sb", [16, 1024], BF16, RO + 8704, ["s3r"])
    bnd_sb = al.at("bnd_sb", [128, 48], F32, RO + 10752, ["s3r"])
    c_all = al.at("c_all", [16, 1024], F32, RO + 10944, ["s3r"])
    cl = ARENA
    S_sb = al.at("S_sb", [128, 512], BF16, cl, ["s3S"]); cl += 1024
    bc_c = al.at("bc_c", [16, 64], F32, cl, ["s3S"]); cl += 256
    cT2 = al.at("cT2", [128, 8], F32, cl, ["s3S"]); cl += 32
    ncT2 = al.at("ncT2", [128, 8], F32, cl, ["s3S"]); cl += 32
    bT2 = al.at("bT2", [128, 8], F32, cl, ["s3S"]); cl += 32
    bbT2 = al.at("bbT2", [128, 8], F32, cl, ["s3S"]); cl += 32
    nbT2 = al.at("nbT2", [128, 8], F32, cl, ["s3S"]); cl += 32
    ecol2 = al.at("ecol2", [128, 8], F32, cl, ["s3S"]); cl += 32
    bC2a = al.at("bC2a", [128, 128], F32, cl, ["s3S"]); cl += 512
    A_blk = [al.at(f"A_blk{i}", [128, 128], BF16, cl + i * 256, ["s3S"]) for i in range(2)]; cl += 512
    Ab_blk = [al.at(f"Ab_blk{i}", [128, 128], BF16, cl + i * 256, ["s3S"]) for i in range(2)]; cl += 512
    tmp_blk = [al.at(f"tmp_blk{i}", [128, 128], BF16, cl + i * 256, ["s3S"]) for i in range(2)]; cl += 512
    e_sb2 = [al.at(f"e_sb{i}", [128, 64], BF16, cl + i * 128, ["s3S"]) for i in range(2)]; cl += 256
    el_sb2 = [al.at(f"el_sb{i}", [128, 64], BF16, cl + i * 128, ["s3S"]) for i in range(2)]; cl += 256
    qb_sb2 = [al.at(f"qb_sb{i}", [128, 64], BF16, cl + i * 128, ["s3S"]) for i in range(2)]; cl += 256
    otm2 = [al.at(f"otm{i}", [128, 64], BF16, cl + i * 128, ["s3S"]) for i in range(2)]; cl += 256
    rcA = al.at("rcA", [80, 64], F32, cl, ["s3S"]); cl += 256
    rcP = al.at("rcP", [80, 64], F32, cl, ["s3S"]); cl += 256
    assert cl <= ARENA + 5376, cl
    hpic = al.at("hpic", [80, 1024], F32, RO + 4608, ["s3hp"])
    AT_all = al.at("AT_all", [128, 8192], BF16, XN_OFF, ["s3x"])
    WT_all = al.at("WT_all", [128, 8192], BF16, XN_OFF + 16384, ["s3x"])

    a1 = ARENA
    wstr = [al.at(f"wstr{i}", [128, 2048], BF16, a1 + i * 4096, ["s1"]) for i in range(2)]
    kh_b = [al.at(f"kh{i}", [128, KVN], BF16, a1 + 8192 + i * 8192, ["s1"]) for i in range(2)]
    vh_b = [al.at(f"vh{i}", [128, KVN], BF16, a1 + 24576 + i * 8192, ["s1"]) for i in range(2)]
    qh_b = al.at("qh", [128, T], BF16, a1 + 40960, ["s1"])
    att_c = al.at("att_c", [128, 16 * T], BF16, a1 + 43008, ["s1"])
    ae_b = [al.at(f"ae{i}", [128, TH], BF16, a1 + 75776 + i * 1024, ["s1"]) for i in range(4)]
    rs_row = al.at("rs_row", [1, T], F32, a1 + 79872, ["s1"])
    wo_b = [al.at(f"wo{i}", [128, 8192], BF16, a1 + 8192 + i * 16384, ["s1wo"]) for i in range(2)]

    ctx = contextlib.ExitStack()
    PS_T = ctx.enter_context(nc.psum_tensor("PS", [128, 4096], F32))
    PS = PS_T.ap()
    PSB = PS.bitcast(BF16)

    def bank(b, rows=128, n=512, off=0):
        return PS[0:rows, b * 512 + off: b * 512 + off + n]

    em = Em(nc, ctx)

    t_x = em.dma("sp", lambda e: e.dma_start(
        out=x_sb[:, :].rearrange("p (c m) -> p c m", m=T),
        in_=xs.rearrange("(c p) m -> p c m", p=128)), "ds")
    for sb, src in [(ident32, ident_h), (blockones2, blockones2_h),
                    (pairind, pairind_h),
                    (identpair, identpair_h), (maskA, maskA_h), (maskP, maskP_h),
                    (convw, convw_h), (dtb, dtb_h), (oddmask, oddmask_h)]:
        em.dma("sp", lambda e, sb=sb, src=src: e.dma_start(out=sb[:, :], in_=src), "ds")
    for sb, src in [(identb, ident_h), (wb_sb, wb_h), (wa_sb, wa_h), (rt_sb, rt_h)]:
        em.dma("pool", lambda e, sb=sb, src=src: e.dma_start(out=sb[:, :], in_=src), "dg")
    for tgt in (ones32, onesb, onesrow32, ones16):
        em.op("pool", lambda e, tgt=tgt: e.memset(tgt[:, :], 1.0))
    for idx, val in enumerate([1e-5, 1e-6, 1.0, 0.5, BIGNEG]):
        em.op("pool", lambda e, idx=idx, val=val: e.memset(fconst[:, idx:idx + 1], val))
        nc.const_aps.register(F32, val, fconst[:, idx:idx + 1])
    em.dma("sp", lambda e: e.dma_start(out=nonce_sb[:, :], in_=nonce_h), "ds")
    t_al = em.dma("sp", lambda e: e.dma_start(out=negA[:, :], in_=alog_h), "ds")
    t_nw = em.dma("sp", lambda e: e.dma_start(out=nw1p[:, :], in_=nw2_h), "ds")
    em.wait("act", t_al, t_nw)
    t1 = em.op("act", lambda e: e.activation(negA[:, :], negA[:, :], AF.Exp))
    em.wait("dve", t1)
    em.op("dve", lambda e: e.tensor_scalar_mul(negA[:, :], negA[:, :], -1.0))
    em.op("act", lambda e: e.add(nw1p[:, :], nw1p[:, :], 1.0))
    em.barrier()

    def emit_layernorm():
        t_x_now = em.tok("dve")
        em.wait("pe", em.tok("dve"), em.tok("ds"))
        for c in range(NCHUNK):
            for half in range(2):
                t_mean = em.op("pe", lambda e, c=c, h=half: e.matmul(
                    bank(6 + h, rows=1), ones32[:, :],
                    x_sb[:, c * T + h * TH: c * T + h * TH + TH],
                    start=(c == 0), stop=(c == NCHUNK - 1)))
        em.wait("act", t_mean)
        t_m1 = em.op("act", lambda e: e.activation(m_row[0:1, 0:TH], bank(6, rows=1), AF.Copy, scale=1.0 / D))
        t_m2 = em.op("act", lambda e: e.activation(m_row[0:1, TH:T], bank(7, rows=1), AF.Copy, scale=1.0 / D))
        em.wait("pe", t_m1, t_m2)
        for half in range(2):
            t_rep = em.op("pe", lambda e, h=half: e.matmul(
                bank(4 + h), onesrow32[:, :], m_row[0:1, h * TH:h * TH + TH], start=True, stop=True))
        em.wait("dve", t_rep, em.tok("ds"))
        t_xm = []
        for c in range(NCHUNK):
            for half in range(2):
                t_xm.append(em.op("dve", lambda e, c=c, h=half: e.tensor_sub(
                    xn_sb[:, c * T + h * TH: c * T + h * TH + TH],
                    x_sb[:, c * T + h * TH: c * T + h * TH + TH], bank(4 + h))))
        t_last_xm = t_xm[-1]
        t_sq_prev = None
        for c in range(NCHUNK):
            em.wait("dve", t_xm[2 * c + 1])
            t_sq = em.op("dve", lambda e, c=c: e.tensor_mul(
                sq_sc[:, :], xn_sb[:, c * T:(c + 1) * T], xn_sb[:, c * T:(c + 1) * T]))
            em.wait("pe", t_sq)
            for half in range(2):
                t_v = em.op("pe", lambda e, c=c, h=half: e.matmul(
                    bank(6 + h, rows=1), ones32[:, :], sq_sc[:, h * TH:h * TH + TH],
                    start=(c == 0), stop=(c == NCHUNK - 1)))
            em.wait("dve", t_v)
            t_sq_prev = t_sq
        em.wait("act", t_v)
        t_r = em.op("act", lambda e: e.activation(rstd_row[0:1, 0:TH], bank(6, rows=1), AF.Ln, scale=1.0 / D, bias=1e-5))
        t_r = em.op("act", lambda e: e.activation(rstd_row[0:1, TH:T], bank(7, rows=1), AF.Ln, scale=1.0 / D, bias=1e-5))
        em.wait("act", t_r)
        t_r = em.op("act", lambda e: e.activation(rstd_row[0:1, :], rstd_row[0:1, :], AF.Exp, scale=-0.5))
        em.wait("pe", t_r, t_last_xm)
        for half in range(2):
            t_rr = em.op("pe", lambda e, h=half: e.matmul(
                bank(4 + h), onesrow32[:, :], rstd_row[0:1, h * TH:h * TH + TH], start=True, stop=True))
        em.wait("dve", t_rr)
        tfin = []
        for c in range(NCHUNK):
            for half in range(2):
                tfin.append(em.op("dve", lambda e, c=c, h=half: e.tensor_mul(
                    xn_sb[:, c * T + h * TH: c * T + h * TH + TH],
                    xn_sb[:, c * T + h * TH: c * T + h * TH + TH], bank(4 + h))))
        return tfin

    def emit_attn_a():
        last_use = {"wq": [TOK0, TOK0], "kh": [TOK0, TOK0], "vh": [TOK0, TOK0]}
        dma_toks = {}

        def dma_head(h):
            p = h % 2
            em.wait("pool", last_use["wq"][p], last_use["kh"][p], last_use["vh"][p])
            t_wq = em.dma("pool", lambda e, h=h, p=p: e.dma_start(out=wstr[p][:, :], in_=wq_s[h]), "dg")
            t_kh = em.dma("pool", lambda e, h=h, p=p: e.dma_start(out=kh_b[p][:, :], in_=kT_s[h]), "dg")
            t_vh = em.dma("pool", lambda e, h=h, p=p: e.dma_start(out=vh_b[p][:, :], in_=v_s[h]), "dg")
            dma_toks[h] = (t_wq, t_kh, t_vh)

        dma_head(0)
        dma_head(1)
        t_concat = []
        sc_consumed = [TOK0] * 6
        ae_consumed = [TOK0] * 4
        state = {"git": 0, "qh_evac": None, "norm": TOK0, "rs": TOK0}

        def emit_q(h):
            p = h % 2
            t_wq = dma_toks[h][0]
            slots = []
            for half in range(2):
                sb = state["git"] % 6
                state["git"] += 1
                slots.append(sb)
                em.wait("pe", t_wq, sc_consumed[sb])
                for c in range(NCHUNK):
                    tq = em.op("pe", lambda e, p=p, c=c, hf=half, sb=sb: e.matmul(
                        bank(sb), wstr[p][:, c * 128:(c + 1) * 128],
                        xn_sb[:, c * T + hf * TH: c * T + hf * TH + TH],
                        start=(c == 0), stop=(c == NCHUNK - 1)))
            last_use["wq"][p] = tq
            em.wait("dve", tq)
            em.op("dve", lambda e, sb=slots[0]: e.tensor_copy(qh_b[:, 0:TH], bank(sb)))
            t_ev = em.op("dve", lambda e, sb=slots[1]: e.tensor_copy(qh_b[:, TH:T], bank(sb)))
            sc_consumed[slots[0]] = t_ev
            sc_consumed[slots[1]] = t_ev
            state["qh_evac"] = t_ev

        emit_q(0)
        for h in range(HEADS):
            p = h % 2
            t_wq, t_kh, t_vh = dma_toks[h]
            t_evacQ = state["qh_evac"]
            for th in range(2):
                t_vm = None
                exp_tok = {}
                ae_of = {}
                SKEW = 2
                for it in range(32 + SKEW):
                    if it < 32:
                        kc = it
                        sb = state["git"] % 6
                        ab = state["git"] % 4
                        state["git"] += 1
                        ae_of[kc] = ab
                        em.wait("pe", t_kh, t_evacQ, sc_consumed[sb])
                        t_s = em.op("pe", lambda e, p=p, kc=kc, sb=sb, th=th: e.matmul(
                            bank(sb), kh_b[p][:, kc * 128:(kc + 1) * 128],
                            qh_b[:, th * TH:th * TH + TH], start=True, stop=True))
                        em.wait("act", t_s, ae_consumed[ab])
                        t_e = em.op("act", lambda e, sb=sb, ab=ab: e.activation(
                            ae_b[ab][:, :], bank(sb), AF.Exp, scale=SCALE))
                        sc_consumed[sb] = t_e
                        exp_tok[kc] = t_e
                    if it >= SKEW:
                        kc = it - SKEW
                        ab = ae_of[kc]
                        em.wait("pe", exp_tok[kc], t_vh)
                        if kc == 0:
                            em.wait("pe", state["norm"], state["rs"])
                        em.op("pe", lambda e, p=p, kc=kc, ab=ab: e.matmul(
                            bank(6), vh_b[p][:, kc * 128:(kc + 1) * 128],
                            ae_b[ab][:, :], start=(kc == 0), stop=(kc == 31)))
                        t_vm = em.op("pe", lambda e, kc=kc, ab=ab: e.matmul(
                            bank(7, rows=1), onesb[:, :],
                            ae_b[ab][:, :], start=(kc == 0), stop=(kc == 31)))
                        ae_consumed[ab] = t_vm
                last_use["kh"][p] = t_vm
                last_use["vh"][p] = t_vm
                if th == 1 and h + 1 < HEADS:
                    emit_q(h + 1)
                em.wait("dve", t_vm)
                t_rs = em.op("dve", lambda e: e.reciprocal(rs_row[0:1, 0:TH], bank(7, rows=1)))
                state["rs"] = t_rs
                sb = state["git"] % 6
                state["git"] += 1
                em.wait("pe", t_rs, sc_consumed[sb])
                t_rep = em.op("pe", lambda e, sb=sb: e.matmul(
                    bank(sb), onesrow32[:, :], rs_row[0:1, 0:TH], start=True, stop=True))
                em.wait("dve", t_rep, state["norm"])
                t_rr = em.op("dve", lambda e, sb=sb: e.tensor_copy(rrep[:, 0:TH], bank(sb)))
                sc_consumed[sb] = t_rr
                em.wait("dve", t_rr)
                t_norm = em.op("dve", lambda e, h=h, th=th: e.tensor_mul(
                    att_c[:, h * T + th * TH: h * T + th * TH + TH],
                    bank(6), rrep[:, 0:TH]))
                state["norm"] = t_norm
                t_concat.append(t_norm)
            if h + 2 < HEADS:
                dma_head(h + 2)

        wo_dma = {}
        wo_last = [TOK0, TOK0]

        def dma_wo(q):
            p = q % 2
            if q < 2:
                em.wait("pool", last_use["kh"][0], last_use["kh"][1],
                        last_use["vh"][0], last_use["vh"][1])
            else:
                em.wait("pool", wo_last[p])
            wo_dma[q] = em.dma("pool", lambda e, q=q, p=p: e.dma_start(
                out=wo_b[p][:, :], in_=wo_s[q]), "dg")

        dma_wo(0)
        dma_wo(1)
        em.wait("pe", *t_concat)
        t_add = []
        for q in range(4):
            p = q % 2
            em.wait("pe", wo_dma[q])
            for ob in range(4):
                oc = q * 4 + ob
                bp = (oc % 2) * 2
                if oc >= 2:
                    em.wait("pe", t_add[2 * (oc - 2) + 1])
                for half in range(2):
                    for c in range(NCHUNK):
                        t_mm = em.op("pe", lambda e, p=p, ob=ob, c=c, hf=half, bp=bp: e.matmul(
                            bank(bp + hf), wo_b[p][:, (c * 4 + ob) * 128:(c * 4 + ob) * 128 + 128],
                            att_c[:, c * T + hf * TH: c * T + hf * TH + TH],
                            start=(c == 0), stop=(c == NCHUNK - 1)))
                wo_last[p] = t_mm
                em.wait("dve", t_mm)
                for half in range(2):
                    t_ax = em.op("dve", lambda e, oc=oc, hf=half, bp=bp: e.tensor_add(
                        x_sb[:, oc * T + hf * TH: oc * T + hf * TH + TH],
                        x_sb[:, oc * T + hf * TH: oc * T + hf * TH + TH], bank(bp + hf)))
                    t_add.append(t_ax)
            if q + 2 < 4:
                dma_wo(q + 2)
        return t_add

    AX = mybir.AxisListType.X

    def emit_ffn():
        t_ei = em.dma("sp", lambda e: e.dma_start(out=expind[:, :], in_=expind_h), "ds")
        em.wait("pe", em.tok("dve"))
        for c in range(NCHUNK):
            for half in range(2):
                t_lg = em.op("pe", lambda e, c=c, h=half: e.matmul(
                    bank(6 + h, rows=16), rt_sb[:, c * 16:(c + 1) * 16],
                    xn_sb[:, c * T + h * TH: c * T + h * TH + TH],
                    start=(c == 0), stop=(c == NCHUNK - 1)))
        em.wait("dve", t_lg)
        t_lge = em.op("dve", lambda e: e.tensor_copy(lg_sb[:, :], PS[0:16, 6 * 512:6 * 512 + 1024]))
        em.wait("pe", t_lge)
        for tt in range(8):
            t_tr = em.op("pe", lambda e, tt=tt: e.transpose(
                PS[:, 0 * 512 + tt * 16: 0 * 512 + tt * 16 + 16],
                lg_sb[0:16, tt * 128:(tt + 1) * 128], ident32[0:16, 0:16]))
        em.wait("act", t_tr)
        t_pr = em.op("act", lambda e: e.activation(pr_t[:, :], bank(0, n=128), AF.Exp))
        em.wait("dve", t_pr)
        t_cmb_tok = []
        for tt in range(8):
            pr = pr_t[:, tt * 16:(tt + 1) * 16]
            cm = cmb_t[:, tt * 16:(tt + 1) * 16]
            t0 = em.op("dve", lambda e, pr=pr: e.reduce_sum(scol[:, :], pr, AX))
            em.wait("dve", t0)
            t0 = em.op("dve", lambda e: e.reciprocal(rcol[:, :], scol[:, :]))
            em.wait("dve", t0)
            t0 = em.op("dve", lambda e, pr=pr: e.tensor_scalar_mul(pr, pr, rcol[:, 0:1]))
            em.wait("dve", t0)
            t0 = em.op("dve", lambda e, pr=pr: e.tensor_copy(y_t[:, :], pr))
            for it in range(4):
                em.wait("dve", t0)
                t0 = em.op("dve", lambda e: e.reduce_max(thcol[:, :], y_t[:, :], AX))
                if it < 3:
                    em.wait("dve", t0)
                    t0 = em.op("dve", lambda e: e.tensor_single_scalar(msk_t[:, :], y_t[:, :], thcol[:, 0:1], ALU.is_ge))
                    em.wait("dve", t0)
                    t0 = em.op("dve", lambda e: e.scalar_tensor_tensor(
                        y_t[:, :], msk_t[:, :], BIGNEG, y_t[:, :], ALU.mult, ALU.add))
            em.wait("dve", t0)
            t0 = em.op("dve", lambda e, pr=pr: e.tensor_single_scalar(msk_t[:, :], pr, thcol[:, 0:1], ALU.is_ge))
            em.wait("dve", t0)
            t0 = em.op("dve", lambda e, pr=pr: e.tensor_mul(y_t[:, :], pr, msk_t[:, :]))
            em.wait("dve", t0)
            t0 = em.op("dve", lambda e: e.reduce_sum(scol[:, :], y_t[:, :], AX))
            em.wait("dve", t0)
            t0 = em.op("dve", lambda e: e.reciprocal(rcol[:, :], scol[:, :]))
            em.wait("dve", t0)
            t0 = em.op("dve", lambda e, cm=cm: e.tensor_scalar_mul(cm, y_t[:, :], rcol[:, 0:1]))
            t_cmb_tok.append(t0)
        em.wait("pe", *t_cmb_tok)
        for tt in range(8):
            t_tc = em.op("pe", lambda e, tt=tt: e.transpose(
                PS[0:16, 512 + tt * 128: 512 + tt * 128 + 128],
                cmb_t[:, tt * 16:(tt + 1) * 16], ident32[:, :]))
        em.wait("dve", t_tc)
        t_cfm = em.op("dve", lambda e: e.tensor_copy(cfm_sb[:, :], PS[0:16, 512:512 + 1024]))

        gu3 = gu_sb[:, :].rearrange("p (dc f) -> p dc f", f=1536)
        dw3 = dw_sb[:, :].rearrange("p (fc d) -> p fc d", d=2048)
        gu_last = [em.tok("pe"), em.tok("pe")]
        dw_last = [em.tok("pe"), em.tok("pe")]
        gu_dma = {}
        dw_dma = {}

        def dma_gu(e_, half):
            em.wait("pool", gu_last[half])
            gu_dma[(e_, half)] = em.dma("pool", lambda e, e_=e_, half=half: e.dma_start(
                out=gu3[:, :, 768 * half:768 * (half + 1)],
                in_=gu_s[e_].rearrange("p (dc f) -> p dc f", f=1536)[:, :, 768 * half:768 * (half + 1)]), "dg")

        def dma_dw(e_, half):
            em.wait("pool", dw_last[half])
            dw_dma[(e_, half)] = em.dma("pool", lambda e, e_=e_, half=half: e.dma_start(
                out=dw3[:, :, 1024 * half:1024 * (half + 1)],
                in_=dw_s[e_].rearrange("p (fc d) -> p fc d", d=2048)[:, :, 1024 * half:1024 * (half + 1)]), "dg")

        dma_gu(0, 0); dma_gu(0, 1); dma_dw(0, 0); dma_dw(0, 1)
        t_down_evac = [em.tok("dve")] * 16
        em.wait("pe", t_pr)
        for ex in range(NE):
            em.wait("pe", t_cfm, t_down_evac[15])
            for half in range(2):
                t_cr = em.op("pe", lambda e, ex=ex, h=half: e.matmul(
                    bank(6 + h), expind[:, ex * 128:(ex + 1) * 128],
                    cfm_sb[:, h * TH:h * TH + TH], start=True, stop=True))
            em.wait("dve", t_cr)
            em.op("dve", lambda e: e.tensor_copy(crep_sb[:, 0:TH], bank(6)))
            t_crep = em.op("dve", lambda e: e.tensor_copy(crep_sb[:, TH:T], bank(7)))
            t_silu_prev, t_h1_prev = None, None
            t_hc = []
            for fc in range(6):
                em.wait("pe", gu_dma[(ex, 0)])
                if t_silu_prev is not None:
                    em.wait("pe", t_silu_prev)
                for half in range(2):
                    for c in range(NCHUNK):
                        t_g = em.op("pe", lambda e, fc=fc, c=c, h=half: e.matmul(
                            bank(0 + h), gu_sb[:, c * 1536 + fc * 128: c * 1536 + fc * 128 + 128],
                            xn_sb[:, c * T + h * TH: c * T + h * TH + TH],
                            start=(c == 0), stop=(c == NCHUNK - 1)))
                em.wait("act", t_g)
                t_silu_prev = em.op("act", lambda e: e.activation(sg_sb[:, :], PS[:, 0:1024], AF.Silu))
                em.wait("pe", gu_dma[(ex, 1)])
                if t_h1_prev is not None:
                    em.wait("pe", t_h1_prev)
                for half in range(2):
                    for c in range(NCHUNK):
                        t_u = em.op("pe", lambda e, fc=fc, c=c, h=half: e.matmul(
                            bank(2 + h), gu_sb[:, c * 1536 + 768 + fc * 128: c * 1536 + 768 + fc * 128 + 128],
                            xn_sb[:, c * T + h * TH: c * T + h * TH + TH],
                            start=(c == 0), stop=(c == NCHUNK - 1)))
                em.wait("dve", t_u, t_silu_prev, t_crep)
                t_h1 = em.op("dve", lambda e, fc=fc: e.tensor_mul(
                    hc_sb[:, fc * 1024:(fc + 1) * 1024], sg_sb[:, :], PS[:, 2 * 512:2 * 512 + 1024]))
                em.wait("dve", t_h1)
                t_h1 = em.op("dve", lambda e, fc=fc: e.tensor_mul(
                    hc_sb[:, fc * 1024:(fc + 1) * 1024], hc_sb[:, fc * 1024:(fc + 1) * 1024], crep_sb[:, :]))
                t_h1_prev = t_h1
                t_hc.append(t_h1)
            gu_last[0] = t_g
            gu_last[1] = t_u
            if ex + 1 < NE:
                dma_gu(ex + 1, 0); dma_gu(ex + 1, 1)
            em.wait("pe", *t_hc)
            for dc in range(NCHUNK):
                bp = 4 + (dc % 2) * 2
                em.wait("pe", dw_dma[(ex, 0 if dc < 8 else 1)])
                if dc >= 2:
                    em.wait("pe", t_down_evac[dc - 2])
                else:
                    em.wait("pe", t_crep)
                for half in range(2):
                    for fc in range(6):
                        t_d = em.op("pe", lambda e, dc=dc, fc=fc, h=half, bp=bp: e.matmul(
                            bank(bp + h), dw_sb[:, fc * 2048 + dc * 128: fc * 2048 + dc * 128 + 128],
                            hc_sb[:, fc * 1024 + h * TH: fc * 1024 + h * TH + TH],
                            start=(fc == 0), stop=(fc == 5)))
                em.wait("dve", t_d)
                for half in range(2):
                    t_da = em.op("dve", lambda e, dc=dc, h=half, bp=bp: e.tensor_add(
                        x_sb[:, dc * T + h * TH: dc * T + h * TH + TH],
                        x_sb[:, dc * T + h * TH: dc * T + h * TH + TH], bank(bp + h)))
                t_down_evac[dc] = t_da
                if dc == 7:
                    dw_last[0] = t_d
                    if ex + 1 < NE:
                        dma_dw(ex + 1, 0)
            dw_last[1] = t_d
            if ex + 1 < NE:
                dma_dw(ex + 1, 1)

    PID = nc.partition_id(engines=[mybir.EngineType.Pool])
    SRC = PID - (PID % 2)

    def emit_exchange(k, sh_t, row_elems, pat, src3, dst3):
        my_off = PID * row_elems
        pr_off = SRC * row_elems
        em.dma("pool", lambda e: e.dma_start(
            out=bass.AP(sh_t, my_off, pat), in_=src3), "dg")
        em.wait("pool", em.tok("dg"))
        em.dma("pool", lambda e, k=k: e.dma_start(
            out=bass.AP(sh_f, PID * 16, [[16, 1], [1, 16]]), in_=nonce_sb[k:k + 1, :]), "dg")
        psem = em.sem["p%d" % k]

        def pollfn(g):
            with g.register(f"rw{k}") as rw, g.register(f"rf{k}") as rf, \
                 g.register(f"rx{k}") as rx, g.register(f"rc{k}") as rc:
                g.reg_load(rx, nonce_sb[k:k + 1, 0:1])
                g.reg_mov(rw, 0)
                g.reg_mov(rc, 1)
                with g.While(rc):
                    g.reg_add(rw, rw, 16)
                    g.dma_start(out=flagrd[0:1, 0:1],
                                in_=bass.AP(sh_f, SRC * 16, [[16, 1], [1, 1]])).then_inc(psem, 16)
                    g.wait_ge(psem, rw)
                    g.reg_load(rf, flagrd[0:1, 0:1])
                    g.reg_sub(rc, rf, rx)
            return g.nop()

        em.op("pool", pollfn)
        em.dma("pool", lambda e: e.dma_start(
            out=dst3, in_=bass.AP(sh_t, pr_off, pat)), "dg")

    def emit_attn_b():
        em.dma("sp", lambda e: e.dma_start(out=hpind[:, :], in_=hpind_h), "ds")
        pairind_b = al.at("pairind_b", [2, 128], BF16, ARENA + 7680, ["s3S"])
        l2r = al.at("l2r", [2, 1024], BF16, ARENA + 5376, ["s3S"])
        bT2x = al.at("bT2x", [128, 8], F32, ARENA + 7424, ["s3S"])
        em.dma("pool", lambda e: e.dma_start(out=pairind_b[:, :], in_=pairind_h), "dg")

        def sz(j):
            return sz_a[:, j * 1024:(j + 1) * 1024] if j < 7 else sz_b[:, :]

        em.wait("pe", em.tok("dve"))
        for c in range(NCHUNK):
            for half in range(2):
                t_b = em.op("pe", lambda e, c=c, h=half: e.matmul(
                    bank(6 + h, rows=16), wb_sb[:, c * 16:(c + 1) * 16],
                    xn_sb[:, c * T + h * TH: c * T + h * TH + TH],
                    start=(c == 0), stop=(c == NCHUNK - 1)))
        em.wait("act", t_b)
        t_beta = em.op("act", lambda e: e.activation(
            beta_sb[:, :], PS[0:16, 6 * 512:6 * 512 + 1024], AF.Sigmoid))
        em.wait("pe", t_beta)
        for c in range(NCHUNK):
            for half in range(2):
                t_g = em.op("pe", lambda e, c=c, h=half: e.matmul(
                    bank(6 + h, rows=16), wa_sb[:, c * 16:(c + 1) * 16],
                    xn_sb[:, c * T + h * TH: c * T + h * TH + TH],
                    start=(c == 0), stop=(c == NCHUNK - 1)))
        em.wait("dve", t_g)
        t0 = em.op("dve", lambda e: e.tensor_scalar_add(
            g_sb[:, :], PS[0:16, 6 * 512:6 * 512 + 1024], dtb[:, 0:1]))
        em.wait("act", t0)
        t1 = em.op("act", lambda e: e.activation(c_all[:, :], g_sb[:, :], AF.Abs))
        em.wait("act", t1)
        t1 = em.op("act", lambda e: e.activation(c_all[:, :], c_all[:, :], AF.Exp, scale=-1.0))
        em.wait("act", t1)
        t1 = em.op("act", lambda e: e.activation(c_all[:, :], c_all[:, :], AF.Ln, bias=1.0))
        em.wait("dve", t1)
        t0 = em.op("dve", lambda e: e.tensor_relu(g_sb[:, :], g_sb[:, :]))
        em.wait("dve", t0)
        t0 = em.op("dve", lambda e: e.tensor_add(g_sb[:, :], g_sb[:, :], c_all[:, :]))
        em.wait("dve", t0)
        t0 = em.op("dve", lambda e: e.tensor_single_scalar(g_sb[:, :], g_sb[:, :], negA[:, 0:1], ALU.mult))
        t_scan = []
        em.wait("dve", t0)
        for n in range(CN):
            t_scan.append(em.op("dve", lambda e, n=n: e.tensor_tensor_scan(
                c_all[:, n * C:(n + 1) * C], ones16[:, :], g_sb[:, n * C:(n + 1) * C],
                0.0, ALU.mult, ALU.add)))

        z_dma = {}
        z_last = [TOK0, TOK0]

        def dma_z(ocz):
            p = ocz % 2
            em.wait("pool", z_last[p])
            z_dma[ocz] = em.dma("pool", lambda e, ocz=ocz, p=p: e.dma_start(
                out=wkv_b[p][:, :], in_=wz_s[ocz]), "dg")

        dma_z(0); dma_z(1)
        t_sz_prev = None
        for ocz in range(8):
            p = ocz % 2
            em.wait("pe", z_dma[ocz])
            if t_sz_prev is not None:
                em.wait("pe", t_sz_prev)
            for half in range(2):
                for c in range(NCHUNK):
                    t_z = em.op("pe", lambda e, p=p, c=c, h=half: e.matmul(
                        bank(2 + h), wkv_b[p][:, c * 128:(c + 1) * 128],
                        xn_sb[:, c * T + h * TH: c * T + h * TH + TH],
                        start=(c == 0), stop=(c == NCHUNK - 1)))
            z_last[p] = t_z
            if ocz + 2 < 8:
                dma_z(ocz + 2)
            em.wait("act", t_z)
            t_sz_prev = em.op("act", lambda e, ocz=ocz: e.activation(
                sz(ocz), PS[:, 2 * 512:2 * 512 + 1024], AF.Silu))

        q_dma = {}

        def dma_qkv(ocq):
            p = ocq % 2
            em.wait("pool", z_last[p] if ocq < 2 else q_dma_last[p])
            q_dma[ocq] = em.dma("pool", lambda e, ocq=ocq, p=p: e.dma_start(
                out=wkv_b[p][:, :], in_=wqkv_s[ocq]), "dg")

        q_dma_last = [TOK0, TOK0]
        dma_qkv(0); dma_qkv(1)
        t_qe_prev = None
        t_bnd = []
        for ocq in range(16):
            p = ocq % 2
            em.wait("pe", q_dma[ocq])
            if t_qe_prev is not None:
                em.wait("pe", t_qe_prev)
            for half in range(2):
                for c in range(NCHUNK):
                    t_q = em.op("pe", lambda e, p=p, c=c, h=half: e.matmul(
                        bank(0 + h), wkv_b[p][:, c * 128:(c + 1) * 128],
                        xn_sb[:, c * T + h * TH: c * T + h * TH + TH],
                        start=(c == 0), stop=(c == NCHUNK - 1)))
            q_dma_last[p] = t_q
            if ocq + 2 < 16:
                dma_qkv(ocq + 2)
            em.wait("dve", t_q)
            em.op("dve", lambda e, ocq=ocq: e.tensor_copy(
                qcraw_sb[:, ocq * 1024:(ocq + 1) * 1024], PS[:, 0:1024]))
            t_qe_prev = em.op("dve", lambda e, ocq=ocq: e.tensor_copy(
                bnd_sb[:, ocq * 3:ocq * 3 + 3], PS[:, 1021:1024]))
            t_bnd.append(t_qe_prev)
        em.barrier()
        if S3PH < 2:
            return

        emit_exchange(0, sh_q, 6144, [[3, 128], [384, 16], [1, 3]],
                      bnd_sb[:, :].rearrange("p (c m) -> p c m", m=3),
                      bnd_sb[:, :].rearrange("p (c m) -> p c m", m=3))
        em.wait("dve", em.tok("dg"))
        t_bm = em.op("dve", lambda e: e.tensor_scalar_mul(bnd_sb[:, :], bnd_sb[:, :], oddmask[:, 0:1]))
        em.barrier()
        if S3PH < 3:
            return

        for oc in range(16):
            em.wait("dve", t_bm)
            t0 = em.op("dve", lambda e, oc=oc: e.tensor_copy(ext_sb[:, 0:3], bnd_sb[:, oc * 3:oc * 3 + 3]))
            em.wait("dve", t0)
            t0 = em.op("dve", lambda e, oc=oc: e.tensor_copy(
                ext_sb[:, 3:1027], qcraw_sb[:, oc * 1024:(oc + 1) * 1024]))
            em.wait("dve", t0)
            t0 = em.op("dve", lambda e, oc=oc: e.tensor_single_scalar(
                acc_sb[:, :], ext_sb[:, 0:1024], convw[:, oc * 4:oc * 4 + 1], ALU.mult))
            for k in range(1, 4):
                em.wait("dve", t0)
                t0 = em.op("dve", lambda e, oc=oc, k=k: e.scalar_tensor_tensor(
                    acc_sb[:, :], ext_sb[:, k:k + 1024], convw[:, oc * 4 + k:oc * 4 + k + 1],
                    acc_sb[:, :], ALU.mult, ALU.add))
            em.wait("act", t0)
            t_cv = em.op("act", lambda e, oc=oc: e.activation(
                qc_sb[:, oc * 1024:(oc + 1) * 1024], acc_sb[:, :], AF.Silu))
            em.wait("dve", t_cv)
        QLN = float(np.log(1.0 / 8.0))
        if not hasattr(nc, "_qln_reg"):
            nc._qln_reg = True
            nc.const_aps.register(F32, QLN, fconst[:, 5:6])
        t_qlnm = em.op("pool", lambda e: e.memset(fconst[:, 5:6], QLN))
        em.wait("act", t_qlnm)
        for kt in range(8):
            em.wait("dve", em.tok("act"))
            t0 = em.op("dve", lambda e, kt=kt: e.tensor_mul(
                acc_sb[:, :], qc_sb[:, kt * 1024:(kt + 1) * 1024], qc_sb[:, kt * 1024:(kt + 1) * 1024]))
            em.wait("pe", t0)
            for half in range(2):
                t_s = em.op("pe", lambda e, h=half: e.matmul(
                    bank(6 + h, rows=2), blockones2[:, :], acc_sb[:, h * TH:h * TH + TH],
                    start=True, stop=True))
            em.wait("act", t_s)
            t_l = em.op("act", lambda e: e.activation(
                PS[0:2, 5 * 512:5 * 512 + 1024], PS[0:2, 6 * 512:6 * 512 + 1024], AF.Ln, bias=1e-6))
            em.wait("act", t_l)
            t_l = em.op("act", lambda e, kt=kt: e.activation(
                l2r[:, :], PS[0:2, 5 * 512:5 * 512 + 1024], AF.Exp, scale=-0.5,
                bias=(QLN if kt < 4 else 0.0)))
            em.wait("pe", t_l)
            for half in range(2):
                t_rp = em.op("pe", lambda e, h=half: e.matmul(
                    bank(0 + h), pairind_b[:, :], l2r[:, h * TH:h * TH + TH], start=True, stop=True))
            em.wait("dve", t_rp)
            t_fin = em.op("dve", lambda e, kt=kt: e.tensor_mul(
                qc_sb[:, kt * 1024:(kt + 1) * 1024], qc_sb[:, kt * 1024:(kt + 1) * 1024],
                PS[:, 0:1024]))
            em.wait("pe", t_fin)
            em.wait("dve", t_fin)
        em.barrier()

        em.wait("sp", em.tok("act"), em.tok("dve"), em.tok("pe"))
        em.dma("sp", lambda e: e.dma_start(out=hpic[:, :], in_=hpic_h), "ds")
        em.wait("dve", em.tok("ds"))
        em.wait("pe", em.tok("ds"))
        em.op("dve", lambda e: e.tensor_copy(rcA[0:64, :], maskA[:, :]))
        em.op("dve", lambda e: e.tensor_copy(rcP[0:64, :], maskP[:, :]))
        for par in range(2):
            em.op("pool", lambda e, par=par: e.memset(A_blk[par][0:64, 64:128], 0.0))
            em.op("pool", lambda e, par=par: e.memset(A_blk[par][64:128, 0:64], 0.0))
            em.op("pool", lambda e, par=par: e.memset(Ab_blk[par][0:64, 64:128], 0.0))
            em.op("pool", lambda e, par=par: e.memset(Ab_blk[par][64:128, 0:64], 0.0))
        em.barrier()

        def kch(j, n):
            kt, rb = 4 + j // 2, 64 * (j % 2)
            return qc_sb[rb:rb + 64, kt * 1024 + n * C: kt * 1024 + n * C + C]

        def qch(j, n):
            kt, rb = j // 2, 64 * (j % 2)
            return qc_sb[rb:rb + 64, kt * 1024 + n * C: kt * 1024 + n * C + C]

        def vch(h, n):
            kt, rb = 8 + h // 2, 64 * (h % 2)
            return qc_sb[rb:rb + 64, kt * 1024 + n * C: kt * 1024 + n * C + C]

        def sl_all(t, n, j):
            off = (n * 8 + j) * 64
            return t[:, off:off + 64]

        if S3PH < 4:
            return
        e8 = al.at("e8", [128, 512], BF16, ARENA, ["s3blk8"])
        el8 = al.at("el8", [128, 512], BF16, ARENA + 2016, ["s3blk8"])
        A4 = al.at("A4", [128, 512], BF16, ARENA + 5376, ["s3blk8"])
        Ab4 = al.at("Ab4", [128, 512], BF16, ARENA + 6400, ["s3blk8"])
        tmp4 = al.at("tmp4", [128, 512], BF16, RO + 2048, ["s3blk8"])
        A4b = al.at("A4b", [128, 512], BF16, RO + 3072, ["s3blk8"])
        tmp4b = al.at("tmp4b", [128, 512], BF16, RO + 15040, ["s3blk8"])
        em.op("pool", lambda e: e.memset(A4[:, :], 0.0))
        em.op("pool", lambda e: e.memset(A4b[:, :], 0.0))
        t_ms8 = em.op("pool", lambda e: e.memset(Ab4[:, :], 0.0))
        em.wait("dve", t_ms8)
        em.wait("pe", t_ms8)

        t_prevC = {"pe": em.tok("pe"), "dve": em.tok("dve"), "act": em.tok("act")}
        for n in range(CN):
            cch = c_all[:, n * C:(n + 1) * C]
            em.wait("pe", t_prevC["dve"], t_prevC["act"])
            t_ct = em.op("pe", lambda e, cch=cch: e.transpose(
                PS[0:64, 6 * 512:6 * 512 + 16], cch, ident32[0:16, 0:16]))
            t_bt = em.op("pe", lambda e, n=n: e.transpose(
                PSB[0:64, 7 * 1024:7 * 1024 + 16], beta_sb[:, n * C:(n + 1) * C], identb[0:16, 0:16]))
            for j in range(8):
                t_r63 = em.op("pe", lambda e, j=j, cch=cch: e.matmul(
                    PS[:, 5 * 512 + j: 5 * 512 + j + 1], hpind[:, j * 128:(j + 1) * 128],
                    cch[:, 63:64], start=True, stop=True))
            em.wait("dve", t_ct, t_bt, t_r63, em.tok("act"))
            em.op("dve", lambda e, cch=cch: e.tensor_copy(rcA[64:80, :], cch))
            em.op("dve", lambda e, cch=cch: e.tensor_copy(rcP[64:80, :], cch))
            tcp = []
            for j in range(8):
                tcp.append(em.op("dve", lambda e, j=j: e.tensor_copy(
                    cT2[0:64, j:j + 1], PS[0:64, 6 * 512 + 2 * j: 6 * 512 + 2 * j + 1])))
                tcp.append(em.op("dve", lambda e, j=j: e.tensor_copy(
                    cT2[64:128, j:j + 1], PS[0:64, 6 * 512 + 2 * j + 1: 6 * 512 + 2 * j + 2])))
                tcp.append(em.op("dve", lambda e, j=j: e.tensor_copy(
                    bT2x[0:64, j:j + 1], PSB[0:64, 7 * 1024 + 2 * j: 7 * 1024 + 2 * j + 1])))
                tcp.append(em.op("dve", lambda e, j=j: e.tensor_copy(
                    bT2x[64:128, j:j + 1], PSB[0:64, 7 * 1024 + 2 * j + 1: 7 * 1024 + 2 * j + 2])))
            em.wait("dve", *tcp)
            t0 = em.op("dve", lambda e: e.tensor_scalar_mul(ncT2[:, :], cT2[:, :], -1.0))
            em.wait("act", t0)
            t_bt2 = em.op("act", lambda e: e.activation(bT2[:, :], cT2[:, :], AF.Exp))
            em.wait("dve", t_bt2)
            t0 = em.op("dve", lambda e: e.tensor_mul(bbT2[:, :], bT2x[:, :], bT2[:, :]))
            em.wait("dve", t0)
            t0 = em.op("dve", lambda e: e.tensor_scalar_mul(nbT2[:, :], bT2x[:, :], -1.0))
            em.wait("dve", t0)
            t_e2 = em.op("dve", lambda e: e.tensor_sub(
                ecol2[:, :], PS[:, 5 * 512:5 * 512 + 8], cT2[:, :]))
            em.wait("act", t_e2)
            t_bc2 = em.op("act", lambda e, n=n: e.activation(
                bC2a[:, n * 8:(n + 1) * 8], PS[:, 5 * 512:5 * 512 + 8], AF.Exp))
            t_ec2 = em.op("act", lambda e: e.activation(ecol2[:, :], ecol2[:, :], AF.Exp))

            em.wait("pe", t_ec2)
            for j in range(8):
                for rb in (0, 64):
                    em.op("pe", lambda e, j=j, n=n, rb=rb: e.matmul(
                        PS[rb:rb + 64, j * 64:j * 64 + 64], kch(j, n), qch(j, n),
                        start=True, stop=True))
                    em.op("pe", lambda e, j=j, n=n, rb=rb: e.matmul(
                        PS[rb:rb + 64, 512 + j * 64:512 + j * 64 + 64], kch(j, n), kch(j, n),
                        start=True, stop=True))
                em.op("pe", lambda e, j=j: e.matmul(
                    PS[:, 2 * 512 + j * 64:2 * 512 + j * 64 + 64],
                    hpic[:, j * 128:(j + 1) * 128], rcA[:, :], start=True, stop=True))
                em.op("pe", lambda e, j=j: e.matmul(
                    PS[:, 3 * 512 + j * 64:3 * 512 + j * 64 + 64],
                    hpic[:, j * 128:(j + 1) * 128], rcP[:, :], start=True, stop=True))
            t_w1m = em.tok("pe")
            for j in range(8):
                rbk = 64 * (j % 2)
                kb = 4 + (j % 2)
                vb = 6 + (j % 2)
                for rb in (0, 64):
                    em.op("pe", lambda e, j=j, n=n, rb=rb, kb=kb, rbk=rbk: e.transpose(
                        PSB[rb:rb + 64, kb * 1024 + (j // 2) * 64: kb * 1024 + (j // 2) * 64 + 64],
                        kch(j, n), identb[rbk:rbk + 64, rbk:rbk + 64]))
                em.op("pe", lambda e, j=j, n=n, vb=vb: e.transpose(
                    PSB[0:64, vb * 1024 + (j // 2) * 64: vb * 1024 + (j // 2) * 64 + 64],
                    vch(2 * j, n), identb[0:64, 0:64]))
                em.op("pe", lambda e, j=j, n=n, vb=vb: e.transpose(
                    PSB[64:128, vb * 1024 + (j // 2) * 64: vb * 1024 + (j // 2) * 64 + 64],
                    vch(2 * j + 1, n), identb[64:128, 64:128]))
            t_w1 = em.tok("pe")

            em.wait("act", t_w1m)
            for j in range(8):
                em.op("act", lambda e, j=j: e.activation(
                    e8[:, j * 64:(j + 1) * 64], PS[:, 2 * 512 + j * 64:2 * 512 + j * 64 + 64],
                    AF.Exp, bias=ncT2[:, j:j + 1]))
                em.op("act", lambda e, j=j: e.activation(
                    el8[:, j * 64:(j + 1) * 64], PS[:, 3 * 512 + j * 64:3 * 512 + j * 64 + 64],
                    AF.Exp, scale=-1.0, bias=cT2[:, j:j + 1]))
            t_w2 = em.tok("act")

            em.wait("dve", t_w2, t_w1)
            em.op("dve", lambda e, n=n: e.tensor_mul(
                AT_all[:, n * 512:(n + 1) * 512], e8[:, :], PS[:, 0:512]))
            def emit_w3(grp, At, Tt):
                for j in range(4 * grp, 4 * grp + 4):
                    jj = j - 4 * grp
                    for rb in (0, 64):
                        em.op("dve", lambda e, rb=rb, j=j, jj=jj, At=At: e.scalar_tensor_tensor(
                            At[rb:rb + 64, jj * 128 + rb:jj * 128 + rb + 64],
                            el8[rb:rb + 64, j * 64:j * 64 + 64],
                            nbT2[rb:rb + 64, j:j + 1],
                            PS[rb:rb + 64, 512 + j * 64:512 + j * 64 + 64],
                            ALU.mult, ALU.mult))
                    kb = 4 + (j % 2)
                    vb = 6 + (j % 2)
                    em.op("dve", lambda e, j=j, jj=jj, vb=vb, Tt=Tt: e.tensor_single_scalar(
                        Tt[:, jj * 128:jj * 128 + 64],
                        PSB[:, vb * 1024 + (j // 2) * 64: vb * 1024 + (j // 2) * 64 + 64],
                        bT2x[:, j:j + 1], ALU.mult))
                    em.op("dve", lambda e, j=j, jj=jj, kb=kb, Tt=Tt: e.tensor_single_scalar(
                        Tt[:, jj * 128 + 64:jj * 128 + 128],
                        PSB[:, kb * 1024 + (j // 2) * 64: kb * 1024 + (j // 2) * 64 + 64],
                        bbT2[:, j:j + 1], ALU.mult))
                    em.op("dve", lambda e, j=j, n=n, kb=kb: e.tensor_single_scalar(
                        sl_all(K2_all, n, j),
                        PSB[:, kb * 1024 + (j // 2) * 64: kb * 1024 + (j // 2) * 64 + 64],
                        ecol2[:, j:j + 1], ALU.mult))
                return em.tok("dve")

            A0TBC = [([0, 2, 3, 0], [768, 768, 768, 832]),
                     ([4, 5, 6, 7], [768, 768, 768, 768])]

            def emit_a0t(grp, At):
                A0TB, A0TC = A0TBC[grp]
                for jj in range(4):
                    for rb in (0, 64):
                        em.op("pe", lambda e, rb=rb, jj=jj, At=At, A0TB=A0TB, A0TC=A0TC: e.transpose(
                            PSB[rb:rb + 64, A0TB[jj] * 1024 + A0TC[jj]: A0TB[jj] * 1024 + A0TC[jj] + 64],
                            At[rb:rb + 64, jj * 128 + rb:jj * 128 + rb + 64],
                            identb[rb:rb + 64, rb:rb + 64]))
                return em.tok("pe")

            def emit_abevac(grp):
                A0TB, A0TC = A0TBC[grp]
                for jj in range(4):
                    for rb in (0, 64):
                        em.op("dve", lambda e, rb=rb, jj=jj, A0TB=A0TB, A0TC=A0TC: e.tensor_copy(
                            Ab4[rb:rb + 64, jj * 128 + rb:jj * 128 + rb + 64],
                            PSB[rb:rb + 64, A0TB[jj] * 1024 + A0TC[jj]: A0TB[jj] * 1024 + A0TC[jj] + 64]))
                return em.tok("dve")

            def emit_lvl_pe(lvl, At, Tt):
                for jj in range(4):
                    em.op("pe", lambda e, jj=jj, Tt=Tt: e.matmul(
                        PS[:, 2 * 512 + jj * 128: 2 * 512 + jj * 128 + 128],
                        Ab4[:, jj * 128:(jj + 1) * 128], Tt[:, jj * 128:(jj + 1) * 128],
                        start=True, stop=True))
                if lvl < 5:
                    for jj in range(4):
                        em.op("pe", lambda e, jj=jj, At=At: e.matmul(
                            PS[:, 3 * 512 + jj * 128: 3 * 512 + jj * 128 + 128],
                            Ab4[:, jj * 128:(jj + 1) * 128], At[:, jj * 128:(jj + 1) * 128],
                            start=True, stop=True))
                    for jj in range(4):
                        em.op("pe", lambda e, jj=jj, At=At: e.matmul(
                            PS[:, 0 * 512 + jj * 128: 0 * 512 + jj * 128 + 128],
                            At[:, jj * 128:(jj + 1) * 128], Ab4[:, jj * 128:(jj + 1) * 128],
                            start=True, stop=True))
                return em.tok("pe")

            def emit_lvl_dve(lvl, At, Tt):
                t_ta = em.op("dve", lambda e, Tt=Tt: e.tensor_add(
                    Tt[:, :], Tt[:, :], PS[:, 2 * 512:3 * 512]))
                if lvl < 5:
                    em.op("dve", lambda e, At=At: e.tensor_copy(At[:, :], PS[:, 3 * 512:4 * 512]))
                    t_ta = em.op("dve", lambda e: e.tensor_copy(Ab4[:, :], PS[:, 0:512]))
                return t_ta

            def emit_u(grp, Tt):
                return em.op("dve", lambda e, n=n, grp=grp, Tt=Tt: e.tensor_copy(
                    U_all[:, n * 512 + grp * 256:n * 512 + grp * 256 + 256].rearrange(
                        "p (j m) -> p j m", m=64),
                    Tt[:, :].rearrange("p (j m) -> p j m", m=128)[:, :, 0:64]))

            def emit_wt_tr(Tt):
                for jj in range(4):
                    em.op("pe", lambda e, jj=jj, Tt=Tt: e.transpose(
                        PSB[0:64, (4 + jj) * 1024: (4 + jj) * 1024 + 64],
                        Tt[0:64, jj * 128 + 64:jj * 128 + 128], identb[0:64, 0:64]))
                    em.op("pe", lambda e, jj=jj, Tt=Tt: e.transpose(
                        PSB[64:128, (4 + jj) * 1024: (4 + jj) * 1024 + 64],
                        Tt[64:128, jj * 128 + 64:jj * 128 + 128], identb[64:128, 64:128]))
                return em.tok("pe")

            def emit_wt_evac(grp):
                t = None
                for jj in range(4):
                    t = em.op("dve", lambda e, jj=jj, n=n, grp=grp: e.tensor_copy(
                        WT_all[:, (n * 8 + grp * 4 + jj) * 64:(n * 8 + grp * 4 + jj) * 64 + 64],
                        PSB[:, (4 + jj) * 1024: (4 + jj) * 1024 + 64]))
                return t

            t_w3g0 = emit_w3(0, A4, tmp4)
            em.wait("pe", t_w3g0)
            t_w4g0 = emit_a0t(0, A4)
            em.wait("dve", t_w4g0)
            t_w5g0 = emit_abevac(0)
            t_w3g1 = emit_w3(1, A4b, tmp4b)
            t_lvl = t_w5g0
            t_w4g1 = None
            for lvl in range(6):
                em.wait("pe", t_lvl)
                t_mm = emit_lvl_pe(lvl, A4, tmp4)
                if lvl == 0:
                    em.wait("pe", t_w3g1)
                    t_w4g1 = emit_a0t(1, A4b)
                em.wait("dve", t_mm)
                t_lvl = emit_lvl_dve(lvl, A4, tmp4)
            em.wait("dve", t_mm, t_w4g1)
            t_w5g1 = emit_abevac(1)
            t_u0 = emit_u(0, tmp4)
            t_lvl = t_w5g1
            t_w9g0 = None
            for lvl in range(6):
                em.wait("pe", t_lvl)
                t_mm = emit_lvl_pe(lvl, A4b, tmp4b)
                if lvl == 0:
                    em.wait("pe", t_u0)
                    t_w8g0 = emit_wt_tr(tmp4)
                em.wait("dve", t_mm)
                t_lvl = emit_lvl_dve(lvl, A4b, tmp4b)
                if lvl == 0:
                    em.wait("dve", t_w8g0)
                    t_w9g0 = emit_wt_evac(0)
            t_u1 = emit_u(1, tmp4b)
            em.wait("pe", t_u1, t_w9g0)
            emit_wt_tr(tmp4b)
            t_w8g1 = em.tok("pe")
            em.wait("dve", t_w8g1)
            t_w9 = emit_wt_evac(1)
            t_prevC = {"pe": em.tok("pe"), "dve": t_w9, "act": em.tok("act")}
            em.wait("pe", t_w9)
        em.barrier()
        if os.environ.get('S3DBG'):
            for i, tsb in enumerate((U_all, WT_all, K2_all, AT_all)):
                em.dma("pool", lambda e, i=i, tsb=tsb: e.dma_start(
                    out=dbg_d[i], in_=tsb[:, :]), "dg")
            em.dma("pool", lambda e: e.dma_start(out=dbg_d[4][:, 0:4096],
                in_=qc_sb[:, 0:4096]), "dg")
            em.dma("pool", lambda e: e.dma_start(out=dbg_d[4][:, 4096:8192],
                in_=qc_sb[:, 4096:8192]), "dg")
            em.dma("pool", lambda e: e.dma_start(out=dbg_d[5][0:16, 0:1024], in_=c_all[:, :]), "dg")
            em.dma("pool", lambda e: e.dma_start(out=dbg_d[5][0:16, 1024:2048], in_=beta_sb[:, :]), "dg")
            em.dma("pool", lambda e: e.dma_start(out=dbg_d[5][0:128, 2048:2176], in_=bC2a[:, :]), "dg")
            em.wait("pool", em.tok("dg"))
        em.barrier()

        def emit_chain(save):
            bank_last = [em.tok("dve")] * 4
            for n in range(CN):
                for j in range(8):
                    pb = (j % 2) * 2
                    em.wait("pe", bank_last[pb], bank_last[pb + 1])
                    for rb in (0, 64):
                        t_p = em.op("pe", lambda e, n=n, j=j, rb=rb, pb=pb: e.matmul(
                            PS[rb:rb + 64, pb * 512:pb * 512 + 64],
                            sl_all(WT_all, n, j)[rb:rb + 64, :],
                            S_sb[rb:rb + 64, j * 64:(j + 1) * 64], start=True, stop=True))
                    em.wait("dve", t_p)
                    if save:
                        em.op("dve", lambda e, n=n, j=j: e.tensor_copy(
                            sl_all(S_all, n, j), S_sb[:, j * 64:(j + 1) * 64]))
                        t_pp = em.op("dve", lambda e, n=n, j=j, pb=pb: e.tensor_sub(
                            sl_all(U_all, n, j), sl_all(U_all, n, j),
                            PS[:, pb * 512:pb * 512 + 64]))
                        P_ap = sl_all(U_all, n, j)
                    else:
                        t_pp = em.op("dve", lambda e, n=n, j=j, pb=pb: e.tensor_sub(
                            tmp_blk[0][:, 0:64], sl_all(U_all, n, j),
                            PS[:, pb * 512:pb * 512 + 64]))
                        P_ap = tmp_blk[0][:, 0:64]
                    em.wait("pe", t_pp)
                    for rb in (0, 64):
                        t_su = em.op("pe", lambda e, n=n, j=j, rb=rb, pb=pb, P_ap=P_ap: e.matmul(
                            PS[rb:rb + 64, (pb + 1) * 512:(pb + 1) * 512 + 64],
                            sl_all(K2_all, n, j)[rb:rb + 64, :], P_ap[rb:rb + 64, :],
                            start=True, stop=True))
                    em.wait("dve", t_su)
                    t_s = em.op("dve", lambda e, n=n, j=j, pb=pb: e.scalar_tensor_tensor(
                        S_sb[:, j * 64:(j + 1) * 64], S_sb[:, j * 64:(j + 1) * 64],
                        bC2a[:, n * 8 + j:n * 8 + j + 1],
                        PS[:, (pb + 1) * 512:(pb + 1) * 512 + 64], ALU.mult, ALU.add))
                    bank_last[pb] = t_pp
                    bank_last[pb + 1] = t_s

        if S3PH < 5:
            return
        t0 = em.op("pool", lambda e: e.memset(S_sb[:, :], 0.0))
        em.wait("dve", t0)
        em.wait("pe", t0)
        emit_chain(save=False)
        em.barrier()
        emit_exchange(1, sh_s, 65536, [[64, 128], [8192, 8], [1, 64]],
                      S_sb[:, :].rearrange("p (j m) -> p j m", m=64),
                      S_sb[:, :].rearrange("p (j m) -> p j m", m=64))
        em.wait("dve", em.tok("dg"))
        em.op("dve", lambda e: e.tensor_scalar_mul(S_sb[:, :], S_sb[:, :], oddmask[:, 0:1]))
        em.barrier()
        emit_chain(save=True)
        em.barrier()

        if S3PH < 6:
            return
        par_last = [em.tok("dve"), em.tok("dve")]
        t_bcc_prev = None
        for n in range(CN):
            cch = c_all[:, n * C:(n + 1) * C]
            em.wait("act", em.tok("pe"))
            t_bcc = em.op("act", lambda e, cch=cch: e.activation(bc_c[:, :], cch, AF.Exp))
            for j in range(8):
                par = j % 2
                h1, h2 = 2 * j, 2 * j + 1
                em.wait("pe", t_bcc, par_last[par])
                t_rb = em.op("pe", lambda e, j=j, par=par: e.matmul(
                    PS[:, 6 * 512 + par * 64:6 * 512 + par * 64 + 64],
                    hpind[:, j * 128:(j + 1) * 128], bc_c[:, :], start=True, stop=True))
                em.wait("dve", t_rb)
                for rb, hh in ((0, h1), (64, h2)):
                    em.op("dve", lambda e, rb=rb, n=n, j=j, par=par: e.tensor_mul(
                        qb_sb2[par][rb:rb + 64, :], qch(j, n),
                        PS[rb:rb + 64, 6 * 512 + par * 64:6 * 512 + par * 64 + 64]))
                t_qb = em.tok("dve")
                em.wait("pe", t_qb)
                for rb in (0, 64):
                    em.op("pe", lambda e, rb=rb, n=n, j=j, par=par: e.matmul(
                        PS[rb:rb + 64, par * 512:par * 512 + 64], qb_sb2[par][rb:rb + 64, :],
                        sl_all(S_all, n, j)[rb:rb + 64, :], start=True, stop=False))
                    em.op("pe", lambda e, rb=rb, n=n, j=j, par=par: e.matmul(
                        PS[rb:rb + 64, par * 512:par * 512 + 64],
                        sl_all(AT_all, n, j)[rb:rb + 64, :],
                        sl_all(U_all, n, j)[rb:rb + 64, :], start=False, stop=True))
                t_o = em.tok("pe")
                em.wait("dve", t_o)
                t_ev = em.op("dve", lambda e, par=par: e.tensor_copy(
                    otm2[par][:, :], PS[:, par * 512:par * 512 + 64]))
                em.wait("pe", t_ev)
                for rb in (0, 64):
                    em.op("pe", lambda e, rb=rb, par=par: e.transpose(
                        PSB[rb:rb + 64, (2 + par) * 1024:(2 + par) * 1024 + 64],
                        otm2[par][rb:rb + 64, :], identb[rb:rb + 64, rb:rb + 64]))
                t_tr = em.tok("pe")
                em.wait("dve", t_tr)
                for rb in (0, 64):
                    t_of = em.op("dve", lambda e, rb=rb, n=n, j=j, par=par: e.tensor_copy(
                        o_fm[rb:rb + 64, j * 1024 + n * C: j * 1024 + n * C + C],
                        PSB[rb:rb + 64, (2 + par) * 1024:(2 + par) * 1024 + 64]))
                par_last[par] = t_of
        em.barrier()

        if S3PH < 7:
            return
        for j in range(8):
            em.wait("dve", em.tok("pe"))
            t0 = em.op("dve", lambda e, j=j: e.tensor_mul(
                acc_sb[:, :], o_fm[:, j * 1024:(j + 1) * 1024], o_fm[:, j * 1024:(j + 1) * 1024]))
            em.wait("pe", t0)
            for half in range(2):
                t_s = em.op("pe", lambda e, h=half: e.matmul(
                    bank(6 + h, rows=2), blockones2[:, :], acc_sb[:, h * TH:h * TH + TH],
                    start=True, stop=True))
            em.wait("act", t_s)
            t_l = em.op("act", lambda e: e.activation(
                PS[0:2, 5 * 512:5 * 512 + 1024], PS[0:2, 6 * 512:6 * 512 + 1024], AF.Ln,
                scale=1.0 / 64.0, bias=1e-6))
            em.wait("act", t_l)
            t_l = em.op("act", lambda e: e.activation(
                l2r[:, :], PS[0:2, 5 * 512:5 * 512 + 1024], AF.Exp, scale=-0.5))
            em.wait("pe", t_l)
            for half in range(2):
                t_rp = em.op("pe", lambda e, h=half: e.matmul(
                    bank(0 + h), pairind_b[:, :], l2r[:, h * TH:h * TH + TH], start=True, stop=True))
            em.wait("dve", t_rp)
            t0 = em.op("dve", lambda e, j=j: e.tensor_mul(
                o_fm[:, j * 1024:(j + 1) * 1024], o_fm[:, j * 1024:(j + 1) * 1024], PS[:, 0:1024]))
            em.wait("dve", t0)
            t0 = em.op("dve", lambda e, j=j: e.tensor_single_scalar(
                o_fm[:, j * 1024:(j + 1) * 1024], o_fm[:, j * 1024:(j + 1) * 1024],
                nw1p[:, 0:1], ALU.mult))
            em.wait("dve", t0)
            t0 = em.op("dve", lambda e, j=j: e.tensor_mul(
                o_fm[:, j * 1024:(j + 1) * 1024], o_fm[:, j * 1024:(j + 1) * 1024], sz(j)))
            em.wait("pe", t0)
        wo_dma2 = {}
        wo_last2 = [em.tok("pe"), em.tok("pe")]

        def dma_wout(ocw):
            p = ocw % 2
            em.wait("pool", wo_last2[p])
            wo_dma2[ocw] = em.dma("pool", lambda e, ocw=ocw, p=p: e.dma_start(
                out=wout_b[p][:, :], in_=wout_s[ocw]), "dg")

        dma_wout(0); dma_wout(1)
        t_addf = {}
        for ocw in range(16):
            p = ocw % 2
            bp = (ocw % 2) * 2 + 2
            em.wait("pe", wo_dma2[ocw])
            if ocw >= 2:
                em.wait("pe", t_addf[ocw - 2])
            for half in range(2):
                for vc in range(8):
                    t_mm = em.op("pe", lambda e, p=p, vc=vc, hf=half, bp=bp: e.matmul(
                        bank(bp + hf), wout_b[p][:, vc * 128:(vc + 1) * 128],
                        o_fm[:, vc * 1024 + hf * TH: vc * 1024 + hf * TH + TH],
                        start=(vc == 0), stop=(vc == 7)))
            wo_last2[p] = t_mm
            if ocw + 2 < 16:
                dma_wout(ocw + 2)
            em.wait("dve", t_mm)
            for half in range(2):
                t_ax = em.op("dve", lambda e, ocw=ocw, hf=half, bp=bp: e.tensor_add(
                    x_sb[:, ocw * T + hf * TH: ocw * T + hf * TH + TH],
                    x_sb[:, ocw * T + hf * TH: ocw * T + hf * TH + TH], bank(bp + hf)))
            t_addf[ocw] = t_ax

    emit_layernorm()
    em.barrier()
    emit_attn_a()
    em.barrier()
    if STAGES >= 2:
        emit_layernorm()
        em.barrier()
        emit_ffn()
        em.barrier()
    if STAGES >= 3:
        emit_layernorm()
        em.barrier()
        emit_attn_b()
        em.barrier()
    if STAGES >= 4:
        emit_layernorm()
        em.barrier()
        emit_ffn()
        em.barrier()

    em.wait("sp", em.tok("dve"))
    t_out = em.dma("sp", lambda e: e.dma_start(
        out=out_d.rearrange("(c p) m -> p c m", p=128),
        in_=x_sb[:, :].rearrange("p (c m) -> p c m", m=T)), "ds")
    em.wait("sp", t_out)

    with nc.Block() as block:
        em.replay(block)
    ctx.close()
    return nc


def _prep_shared(i):
    f32 = lambda a: np.ascontiguousarray(np.asarray(a), dtype=np.float32)
    Wq = np.asarray(i["L0_attn_a_Wq"])
    wq_s = f32(Wq.reshape(HEADS, 128, 16, 128).transpose(0, 3, 2, 1).reshape(HEADS, 128, 2048))
    Wo = np.asarray(i["L0_attn_a_Wo"])
    wo_s = f32(Wo.reshape(4, 4, 128, 16, 128).transpose(0, 4, 3, 1, 2).reshape(4, 128, 8192))
    kv_k = np.asarray(i["L0_attn_a_kv_k"]).reshape(KVN, D)
    kT_s = f32(kv_k.T.reshape(HEADS, 128, KVN))
    kv_v = np.asarray(i["L0_attn_a_kv_v"]).reshape(KVN, D)
    v_s = f32(kv_v.reshape(32, 128, 16, 128).transpose(2, 1, 0, 3).reshape(HEADS, 128, KVN))
    Wqkv = np.asarray(i["L0_attn_b_W_qkv"])
    wqkv_s = f32(Wqkv.reshape(16, 128, 16, 128).transpose(0, 3, 2, 1).reshape(16, 128, 2048))
    Wz = np.asarray(i["L0_attn_b_W_z"])
    wz_s = f32(Wz.reshape(8, 128, 16, 128).transpose(0, 3, 2, 1).reshape(8, 128, 2048))
    wb_hh = f32(np.asarray(i["L0_attn_b_W_b"]).reshape(16, 16, 128).transpose(2, 1, 0).reshape(128, 256))
    wa_hh = f32(np.asarray(i["L0_attn_b_W_a"]).reshape(16, 16, 128).transpose(2, 1, 0).reshape(128, 256))
    rt_hh = f32(np.asarray(i["L0_ffn_router_weight"]).reshape(16, 16, 128).transpose(2, 1, 0).reshape(128, 256))
    Wout = np.asarray(i["L0_attn_b_W_out"])
    wout_s = f32(Wout.reshape(16, 128, 8, 128).transpose(0, 3, 2, 1).reshape(16, 128, 1024))
    gup = np.asarray(i["L0_ffn_gate_up_proj"])
    gu_s = f32(gup.reshape(NE, 16, 128, 1536).transpose(0, 2, 1, 3).reshape(NE, 128, 24576))
    dwn = np.asarray(i["L0_ffn_down_proj"])
    dw_s = f32(dwn.reshape(NE, 6, 128, 2048).transpose(0, 2, 1, 3).reshape(NE, 128, 12288))
    convw_hh = f32(np.asarray(i["L0_attn_b_conv_w"]).reshape(16, 128, 4).transpose(1, 0, 2).reshape(128, 64))
    dtb_hh = f32(np.asarray(i["L0_attn_b_dt_bias"]).reshape(16, 1))
    alog_hh = f32(np.asarray(i["L0_attn_b_A_log"]).reshape(16, 1))
    nw2_hh = f32(np.tile(np.asarray(i["L0_attn_b_norm_weight"]).reshape(64), 2).reshape(128, 1))
    ident = np.eye(128, dtype=np.float32)
    blockones2 = np.zeros((128, 2), np.float32)
    blockones2[:64, 0] = 1.0
    blockones2[64:, 1] = 1.0
    pairind = np.zeros((2, 128), np.float32)
    pairind[0, :64] = 1.0
    pairind[1, 64:] = 1.0
    expind = np.zeros((16, 2048), np.float32)
    for e in range(16):
        expind[e, e * 128:(e + 1) * 128] = 1.0
    hpind = np.zeros((16, 1024), np.float32)
    for j in range(8):
        hpind[2 * j, j * 128: j * 128 + 64] = 1.0
        hpind[2 * j + 1, j * 128 + 64: (j + 1) * 128] = 1.0
    hpic = np.zeros((80, 1024), np.float32)
    for j in range(8):
        for k in range(64):
            hpic[k, j * 128 + k] = 1.0
            hpic[k, j * 128 + 64 + k] = 1.0
        hpic[64 + 2 * j, j * 128: j * 128 + 64] = 1.0
        hpic[64 + 2 * j + 1, j * 128 + 64: (j + 1) * 128] = 1.0
    identpair = np.zeros((64, 128), np.float32)
    for k in range(64):
        identpair[k, k] = 1.0
        identpair[k, 64 + k] = 1.0
    ii, jj = np.meshgrid(np.arange(64), np.arange(64), indexing="ij")
    maskA_ = np.where(ii <= jj, 0.0, BIGNEG).astype(np.float32)
    maskP_ = np.where(jj >= ii, -BIGNEG, 0.0).astype(np.float32)
    return dict(wq_s=wq_s, wo_s=wo_s, kT_s=kT_s, v_s=v_s, wqkv_s=wqkv_s, wz_s=wz_s,
                wb_h=wb_hh, wa_h=wa_hh, rt_h=rt_hh, wout_s=wout_s, gu_s=gu_s, dw_s=dw_s,
                convw_h=convw_hh, dtb_h=dtb_hh, alog_h=alog_hh, nw2_h=nw2_hh,
                ident_h=ident, blockones2_h=blockones2, pairind_h=pairind,
                expind_h=expind, hpind_h=hpind, hpic_h=hpic, identpair_h=identpair,
                maskA_h=maskA_, maskP_h=maskP_)


def kernel(**inputs):
    shared = _prep_shared(inputs)
    x = np.asarray(inputs["L0_attn_a_query"], dtype=np.float32)
    base = int(np.random.randint(1, 2 ** 30))
    nonce = np.empty((2, 16), np.int32)
    nonce[0, :] = base + 1
    nonce[1, :] = base + 2
    nc = build_nc()
    in_maps = []
    for c in range(8):
        b, hh = c // 2, c % 2
        m = dict(shared)
        m["xs"] = np.ascontiguousarray(x[b, hh * T:(hh + 1) * T, :].T)
        m["oddmask_h"] = np.full((128, 1), 1.0 if hh == 1 else 0.0, np.float32)
        m["nonce_h"] = nonce
        in_maps.append(m)
    res = run_bass_kernel_spmd(nc, in_maps, core_ids=list(range(8)))
    out = np.empty((4, 2048, 2048), np.float32)
    for c in range(8):
        b, hh = c // 2, c % 2
        out[b, hh * T:(hh + 1) * T, :] = res.results[c]["out"].T
    return out


if __name__ == "__main__":
    import reference as R
    ins = R.setup_inputs()
    o = kernel(**{k: np.asarray(v) for k, v in ins.items()})
    print("ran", o.shape)
